# version 05
# baseline (speedup 1.0000x reference)
import numpy as np
from contextlib import ExitStack
import concourse.bass as bass
import concourse.mybir as mybir
from concourse.alu_op_type import AluOpType as OP
from concourse.bass_utils import run_bass_kernel_spmd

F32 = mybir.dt.float32
BF16 = mybir.dt.bfloat16
I32 = mybir.dt.int32
AF = mybir.ActivationFunctionType
AX = mybir.AxisListType

D = 4096
DC = 32
SEQ = 2048
OWN = 1024
IF = 11008
ICN = 86
H = 8
TBA = 256
NTB = SEQ // TBA
HALF = 512
SBS = [11, 11, 11, 11, 11, 11, 11, 9]
NSB = len(SBS)
INV_SQRT_HD = 1.0 / (512.0 ** 0.5)
EPS = 1e-5


class Ctl:

    def __init__(self, nc, es):
        self.nc = nc
        self.sem = {}
        self.count = {}
        self.hw = {}
        self.es = es

    def new(self, *names):
        for n in names:
            self.sem[n] = self.es.enter_context(self.nc.semaphore(n))
            self.count[n] = 0

    def inc(self, instr, name, n=1):
        self.count[name] += n
        instr.then_inc(self.sem[name], n)
        return (name, self.count[name])

    def tok(self, name):
        return (name, self.count[name])

    def wait(self, eng, tok):
        if tok is None:
            return
        if isinstance(tok, (list, tuple)) and tok and isinstance(tok[0], (list, tuple, type(None))):
            for t in tok:
                self.wait(eng, t)
            return
        name, val = tok
        if val <= 0:
            return
        key = (id(eng), name)
        if self.hw.get(key, 0) >= val:
            return
        eng.wait_ge(self.sem[name], val)
        self.hw[key] = val

    def waits(self, eng, toks):
        for t in toks:
            self.wait(eng, t)

    def dma(self, eng, sem, out, in_, frees=()):
        for f in frees:
            self.wait(eng, f)
        d = eng.dma_start(out=out, in_=in_)
        return self.inc(d, sem, 16)


def build_nc():
    nc = bass.Bass("TRN2", target_bir_lowering=False)
    dram = lambda n, s, ty, k: nc.dram_tensor(n, s, ty, kind=k).ap()
    xt = dram("xt", [D, SEQ], F32, "ExternalInput")
    wqkt = dram("wqkt", [D, 1024], F32, "ExternalInput")
    bqk = dram("bqk", [128, 8], F32, "ExternalInput")
    wvt = dram("wvt", [D, 512], F32, "ExternalInput")
    bvr = dram("bvr", [128, 512], F32, "ExternalInput")
    wot = dram("wot", [512, D], F32, "ExternalInput")
    bo = dram("bo", [128, 32], F32, "ExternalInput")
    gq = dram("gq", [IF, D], I32, "ExternalInput")
    gz = dram("gz", [IF, 32], F32, "ExternalInput")
    gs = dram("gs", [IF, 32], F32, "ExternalInput")
    uq = dram("uq", [IF, D], I32, "ExternalInput")
    uz = dram("uz", [IF, 32], F32, "ExternalInput")
    us = dram("us", [IF, 32], F32, "ExternalInput")
    dqw = dram("dqw", [D, IF], I32, "ExternalInput")
    dzt = dram("dzt", [D, 86], F32, "ExternalInput")
    dst_ = dram("dst", [D, 86], F32, "ExternalInput")
    identd = dram("identd", [128, 128], F32, "ExternalInput")
    onescd = dram("onescd", [128, 1], F32, "ExternalInput")
    onesrd = dram("onesrd", [1, 128], F32, "ExternalInput")
    yt = dram("yt", [D, OWN], F32, "ExternalOutput")
    pscr = dram("pscr", [2, NSB - 1, D, HALF], F32, "Internal")
    wsp_g = dram("wsp_g", [ICN, 2, 128, DC * 128], BF16, "Internal")
    wsp_u = dram("wsp_u", [ICN, 2, 128, DC * 128], BF16, "Internal")
    wspd = dram("wspd", [NSB, DC, 2, 128, 11 * 128], BF16, "Internal")

    with ExitStack() as es:
        C = Ctl(nc, es)
        C.new("pe", "dv", "ac", "bar", "d_const",
              "wqk0", "wqk1", "wv0", "wv1", "wo0", "wo1", "xr0", "xr1",
              "x20", "x21", "wqA0", "wqA1", "zs0", "zs1", "dwq",
              "dzs0", "dzs1", "pl0", "pl1", "oy0", "oy1", "op0", "op1",
              "sp0", "sp1", "spd0", "spd1", "rl0", "rl1", "rld0", "rld1")
        C.new(*[f"xb{i}" for i in range(DC)])

        pe, ve, ae, ge, se = nc.tensor, nc.vector, nc.scalar, nc.gpsimd, nc.sync
        ENGS = [pe, ve, ae, ge, se]
        sbuf = lambda n, s, ty: es.enter_context(nc.sbuf_tensor(n, s, ty))

        ident = sbuf("ident", [128, 128], F32)
        onesc = sbuf("onesc", [128, 1], F32)
        onesr = sbuf("onesr", [1, 128], F32)
        bqk_s = sbuf("bqk_s", [128, 8], F32)
        bvr_s = sbuf("bvr_s", [128, 512], F32)
        bo_s = sbuf("bo_s", [128, 32], F32)
        strow = sbuf("strow", [1, 2048], F32)
        mrrep = sbuf("mrrep", [128, 1024], F32)
        nm_s = sbuf("nm_s", [128, 1], F32)
        rs_s = sbuf("rs_s", [128, 1], F32)
        ri_s = sbuf("ri_s", [128, 1], F32)
        stg = sbuf("stg", [128, 2, 512], F32)

        def barrier():
            names = [n for n in C.count if C.count[n] > 0]
            for e in ENGS:
                for n in names:
                    C.wait(e, (n, C.count[n]))

        for dst_t, src in ((ident, identd), (onesc, onescd), (onesr, onesrd),
                           (bqk_s, bqk), (bvr_s, bvr), (bo_s, bo)):
            d = se.dma_start(out=dst_t[:], in_=src)
            C.inc(d, "d_const", 16)
        tconst = C.tok("d_const")

        esA = ExitStack()
        sbA = lambda n, s, ty: esA.enter_context(nc.sbuf_tensor(n, s, ty))
        q_sb = sbA("q_sb", [128, 4, 1024], F32)
        k_sb = sbA("k_sb", [128, 4, 2048], F32)
        v_sb = sbA("v_sb", [128, 16, 512], F32)
        pvt = sbA("pvt", [128, 4, 1024], F32)

        with ExitStack() as es1:
            sb1 = lambda n, s, ty: es1.enter_context(nc.sbuf_tensor(n, s, ty))
            xblk = sb1("xblk", [128, DC, TBA], F32)
            xln = sb1("xln", [128, DC, TBA], F32)
            sqb = sb1("sqb", [128, 2, TBA], F32)
            tmpb = sb1("tmpb", [128, 2, TBA], F32)
            wqk_b = sb1("wqk_b", [128, 2, 512], F32)
            wv_b = sb1("wv_b", [128, 2, 512], F32)
            ps_stat = es1.enter_context(nc.psum_tensor("ps_stat", [1, 768], F32))
            ps_qk = es1.enter_context(nc.psum_tensor("ps_qk", [128, 4, 512], F32))
            ps_v = es1.enter_context(nc.psum_tensor("ps_v", [128, 2, 512], F32))

            xblk_free = [None] * DC
            xln_free = [None] * DC
            xln_tok = [None] * DC
            sq_free = [None, None]
            tmp_free = [None, None]
            wqk_tok = [None, None]
            wqk_free = [None, None]
            wv_tok = [None, None]
            wv_free = [None, None]
            ps_stat_free = None
            ps_qk_free = None
            ps_v_free = None

            wqk_seq = [(t_, p_, k_) for t_ in range(NTB) for p_ in range(2) for k_ in range(DC)]
            wqk_ptr = [0]

            def load_wqk_next():
                if wqk_ptr[0] >= len(wqk_seq):
                    return
                t_, p_, k_ = wqk_seq[wqk_ptr[0]]
                sl = wqk_ptr[0] % 2
                wqk_ptr[0] += 1
                wqk_tok[sl] = C.dma(se, f"wqk{sl}", wqk_b[:, sl, :],
                                    wqkt[k_ * 128:(k_ + 1) * 128, p_ * 512:(p_ + 1) * 512],
                                    [wqk_free[sl]])

            wv_seq = [(t_, k_) for t_ in range(NTB) for k_ in range(DC)]
            wv_ptr = [0]

            def load_wv_next():
                if wv_ptr[0] >= len(wv_seq):
                    return
                t_, k_ = wv_seq[wv_ptr[0]]
                sl = wv_ptr[0] % 2
                wv_ptr[0] += 1
                wv_tok[sl] = C.dma(se, f"wv{sl}", wv_b[:, sl, :],
                                   wvt[k_ * 128:(k_ + 1) * 128, :], [wv_free[sl]])

            load_wqk_next()
            load_wqk_next()
            load_wv_next()
            load_wv_next()
            wqk_use = [0]
            wv_use = [0]

            for tb in range(NTB):
                xblk_tok = [None] * DC
                for dc in range(DC):
                    xblk_tok[dc] = C.dma(ge, f"xb{dc}", xblk[:, dc, :],
                                         xt[dc * 128:(dc + 1) * 128, tb * TBA:(tb + 1) * TBA],
                                         [xblk_free[dc]])
                C.wait(pe, ps_stat_free)
                for dc in range(DC):
                    sl = dc % 2
                    C.waits(ae, [xblk_tok[dc], sq_free[sl]])
                    a = ae.activation(sqb[:, sl, :], xblk[:, dc, :], AF.Square)
                    tsq = C.inc(a, "ac")
                    C.waits(pe, [xblk_tok[dc], tsq, tconst])
                    pe.matmul(ps_stat[0:1, 0:TBA], onesc[:], xblk[:, dc, :],
                              start=(dc == 0), stop=(dc == DC - 1))
                    m2 = pe.matmul(ps_stat[0:1, 512:512 + TBA], onesc[:], sqb[:, sl, :],
                                   start=(dc == 0), stop=(dc == DC - 1))
                    t = C.inc(m2, "pe")
                    sq_free[sl] = t
                    xblk_free[dc] = [t, tsq]
                tstat = C.tok("pe")
                C.wait(ve, tstat)
                v = ve.tensor_scalar(strow[0:1, 0:TBA], ps_stat[0:1, 0:TBA], 1.0 / D, None, op0=OP.mult)
                C.inc(v, "dv")
                v = ve.tensor_scalar(strow[0:1, 256:256 + TBA], ps_stat[0:1, 512:512 + TBA], 1.0 / D,
                                     None, op0=OP.mult)
                tv = C.inc(v, "dv")
                ps_stat_free = tv
                C.wait(ve, tv)
                v = ve.tensor_tensor(strow[0:1, 512:512 + TBA], strow[0:1, 0:TBA], strow[0:1, 0:TBA],
                                     op=OP.mult)
                tv = C.inc(v, "dv")
                C.wait(ve, tv)
                v = ve.scalar_tensor_tensor(strow[0:1, 768:768 + TBA], strow[0:1, 256:256 + TBA],
                                            EPS, strow[0:1, 512:512 + TBA],
                                            op0=OP.add, op1=OP.subtract)
                tv = C.inc(v, "dv")
                C.wait(ae, tv)
                a = ae.activation(strow[0:1, 1024:1024 + TBA], strow[0:1, 768:768 + TBA], AF.Sqrt)
                ta = C.inc(a, "ac")
                C.wait(ve, ta)
                v = ve.reciprocal(strow[0:1, 1280:1280 + TBA], strow[0:1, 1024:1024 + TBA])
                trstd = C.inc(v, "dv")
                C.waits(pe, [trstd, tconst, ps_v_free])
                pe.matmul(ps_v[:, 0, 0:TBA], onesr[:], strow[0:1, 0:TBA], start=True, stop=True)
                mb = pe.matmul(ps_v[:, 0, 256:256 + TBA], onesr[:], strow[0:1, 1280:1280 + TBA],
                               start=True, stop=True)
                tbb = C.inc(mb, "pe")
                C.wait(ve, tbb)
                v = ve.tensor_copy(mrrep[:, 0:512], ps_v[:, 0, 0:512])
                trep = C.inc(v, "dv")
                tlast_k = None
                for k in range(DC):
                    sl = k % 2
                    C.waits(ve, [xblk_tok[k], trep, tmp_free[sl]])
                    v = ve.tensor_tensor(tmpb[:, sl, :], xblk[:, k, :], mrrep[:, 0:TBA], op=OP.subtract)
                    tv = C.inc(v, "dv")
                    xblk_free[k] = xblk_free[k] + [tv]
                    C.waits(ve, [tv, xln_free[k]])
                    v = ve.tensor_tensor(xln[:, k, :], tmpb[:, sl, :], mrrep[:, 256:256 + TBA], op=OP.mult)
                    txln = C.inc(v, "dv")
                    tmp_free[sl] = txln
                    xln_tok[k] = txln
                    wsl = wqk_use[0] % 2
                    wqk_use[0] += 1
                    vsl = wv_use[0] % 2
                    wv_use[0] += 1
                    C.waits(pe, [txln, wqk_tok[wsl], wv_tok[vsl], ps_qk_free, ps_v_free])
                    for m in range(4):
                        pe.matmul(ps_qk[:, m, 0:TBA], wqk_b[:, wsl, m * 128:(m + 1) * 128],
                                  xln[:, k, :], start=(k == 0), stop=(k == DC - 1))
                    for ts in range(2):
                        mm = pe.matmul(ps_v[:, ts, :], xln[:, k, ts * 128:(ts + 1) * 128],
                                       wv_b[:, vsl, :], start=(k == 0), stop=(k == DC - 1))
                    tlast_k = C.inc(mm, "pe")
                    wqk_free[wsl] = tlast_k
                    wv_free[vsl] = tlast_k
                    load_wqk_next()
                    load_wv_next()
                C.waits(ve, [tlast_k, tconst])
                for ts in range(2):
                    v = ve.tensor_tensor(v_sb[:, tb * 2 + ts, :], ps_v[:, ts, :], bvr_s[:], op=OP.add)
                    C.inc(v, "dv")
                ps_v_free = C.tok("dv")
                for m in range(4):
                    hp = m // 2
                    if m % 2 == 0:
                        if tb < 4:
                            v = ve.tensor_scalar(q_sb[:, hp, tb * TBA:(tb + 1) * TBA], ps_qk[:, m, 0:TBA],
                                                 bqk_s[:, m:m + 1], None, op0=OP.add)
                            C.inc(v, "dv")
                    else:
                        v = ve.tensor_scalar(k_sb[:, hp, tb * TBA:(tb + 1) * TBA], ps_qk[:, m, 0:TBA],
                                             bqk_s[:, m:m + 1], None, op0=OP.add)
                        C.inc(v, "dv")
                ps_qk_free = C.tok("dv")
                tlast_k = None
                for k in range(DC):
                    wsl = wqk_use[0] % 2
                    wqk_use[0] += 1
                    C.waits(pe, [xln_tok[k], wqk_tok[wsl], ps_qk_free])
                    for m in range(4):
                        mm = pe.matmul(ps_qk[:, m, 0:TBA], wqk_b[:, wsl, m * 128:(m + 1) * 128],
                                       xln[:, k, :], start=(k == 0), stop=(k == DC - 1))
                    tlast_k = C.inc(mm, "pe")
                    wqk_free[wsl] = tlast_k
                    xln_free[k] = tlast_k
                    load_wqk_next()
                C.waits(ve, [tlast_k, tconst])
                for m in range(4):
                    mg = m + 4
                    hp = mg // 2
                    if mg % 2 == 0:
                        if tb < 4:
                            v = ve.tensor_scalar(q_sb[:, hp, tb * TBA:(tb + 1) * TBA], ps_qk[:, m, 0:TBA],
                                                 bqk_s[:, mg:mg + 1], None, op0=OP.add)
                            C.inc(v, "dv")
                    else:
                        v = ve.tensor_scalar(k_sb[:, hp, tb * TBA:(tb + 1) * TBA], ps_qk[:, m, 0:TBA],
                                             bqk_s[:, mg:mg + 1], None, op0=OP.add)
                        C.inc(v, "dv")
                ps_qk_free = C.tok("dv")
            barrier()

        sbufR = lambda n, s, ty: es.enter_context(nc.sbuf_tensor(n, s, ty, side="right"))
        wq_b = sbufR("wq_b", [128, 2, 2048], I32)
        zb = sbufR("zb", [128, 2, 32], F32)
        szb = sbufR("szb", [128, 2, 32], F32)
        wtile = sbufR("wtile", [128, 4, 128], F32)
        whiT = sbufR("whiT", [128, 2, DC, 128], BF16)
        wloT = sbufR("wloT", [128, 2, DC, 128], BF16)
        ps_tr = es.enter_context(nc.psum_tensor("ps_tr", [128, 2, 512], F32))
        ring_free = [None, None]
        wq_tok = [None, None]
        wq_free = [None, None]
        zs_tok = [None, None]
        zs_free = [None, None]
        wtile_free = [None] * 4
        wT_tok = [None, None]
        wT_free = [None, None]
        sp_done = {}
        prep_done = [0]

        def strip_load(ic, wi):
            qsrc = gq if wi == 0 else uq
            zsrc = gz if wi == 0 else uz
            ssrc = gs if wi == 0 else us
            for hh in range(2):
                wq_tok[hh] = C.dma(se, f"wqA{hh}", wq_b[:, hh, :],
                                   qsrc[ic * 128:(ic + 1) * 128, hh * 2048:(hh + 1) * 2048],
                                   [wq_free[hh]])
            sl = wi
            C.dma(se, f"zs{sl}", zb[:, sl, :], zsrc[ic * 128:(ic + 1) * 128, :], [zs_free[sl]])
            zs_tok[sl] = C.dma(se, f"zs{sl}", szb[:, sl, :], ssrc[ic * 128:(ic + 1) * 128, :], [])

        def prep_strip_gen():
            strip_load(0, 0)
            for s in range(2 * ICN):
                ic, wi = s // 2, s % 2
                dsl = wi
                zsl = wi
                batches = []
                old_free = wT_free[dsl]
                ttps = []

                def evac(b):
                    psl = b % 2
                    j4 = b * 4
                    C.waits(ae, [ttps[b], old_free])
                    a = ae.activation(whiT[:, dsl, j4:j4 + 4, :], ps_tr[:, psl, :], AF.Copy)
                    thi = C.inc(a, "ac")
                    C.waits(ve, [thi, ttps[b], old_free])
                    v = ve.tensor_tensor(wloT[:, dsl, j4:j4 + 4, :], ps_tr[:, psl, :],
                                         whiT[:, dsl, j4:j4 + 4, :], op=OP.subtract)
                    tlo = C.inc(v, "dv")
                    batches.append([thi, tlo])
                    ring_free[psl] = [thi, tlo]

                for j in range(DC):
                    wsl = j % 4
                    hh = j // 16
                    C.waits(ve, [wq_tok[hh], zs_tok[zsl], wtile_free[wsl]])
                    v = ve.tensor_scalar(wtile[:, wsl, :], wq_b[:, hh, (j % 16) * 128:(j % 16 + 1) * 128],
                                         zb[:, zsl, j:j + 1], szb[:, zsl, j:j + 1],
                                         op0=OP.subtract, op1=OP.mult)
                    tdq = C.inc(v, "dv")
                    if j == 15:
                        wq_free[0] = tdq
                    if j == DC - 1:
                        wq_free[1] = tdq
                        zs_free[zsl] = tdq
                    psl = (j // 4) % 2
                    if j % 4 == 0:
                        C.wait(pe, ring_free[psl])
                    C.waits(pe, [tdq, tconst])
                    mm = pe.transpose(ps_tr[:, psl, (j % 4) * 128:(j % 4 + 1) * 128],
                                      wtile[:, wsl, :], ident[:])
                    ttp = C.inc(mm, "pe")
                    wtile_free[wsl] = ttp
                    if j % 4 == 3:
                        ttps.append(ttp)
                        if len(ttps) >= 2:
                            evac(len(ttps) - 2)
                evac(7)
                wT_tok[dsl] = batches
                nic, nwi = (ic, 1) if wi == 0 else (ic + 1, 0)
                if nic < ICN:
                    strip_load(nic, nwi)
                wsp_w = wsp_g if wi == 0 else wsp_u
                C.dma(ge, f"sp{dsl}", wsp_w[ic, 0], whiT[:, dsl, :, :], [batches[-1][0]])
                tl_sp = C.dma(ge, f"sp{dsl}", wsp_w[ic, 1], wloT[:, dsl, :, :], [batches[-1][1]])
                wT_free[dsl] = [tl_sp] + batches[-1]
                sp_done[s] = tl_sp
                prep_done[0] = s + 1
                yield

        prep_gen = prep_strip_gen()

        def drive_prep(n=1):
            for _ in range(n):
                if next(prep_gen, "END") == "END":
                    return

        with ExitStack() as es2:
            sb2 = lambda n, s, ty: es2.enter_context(nc.sbuf_tensor(n, s, ty))
            P_sb = sb2("P_sb", [128, 2048], F32)
            P2_sb = sb2("P2_sb", [128, 2048], F32)
            PT_sb = sb2("PT_sb", [128, 16, 128], F32)
            ps_s = es2.enter_context(nc.psum_tensor("ps_s", [128, 2048], F32))
            ps_pv = es2.enter_context(nc.psum_tensor("ps_pv", [128, 2, 512], F32))

            ps_s_free = None
            ps_pv_free = [None, None]
            P_free = None
            P2_free = None
            PT_free = None
            nm_free = None
            rs_free = None
            ri_free = None
            it = 0
            for h in range(H):
                bp = (h % 2) * 64
                hp = h // 2
                for qb in range(8):
                    C.wait(pe, ps_s_free)
                    for kt in range(4):
                        mm = pe.matmul(ps_s[:, kt * 512:(kt + 1) * 512],
                                       q_sb[bp:bp + 64, hp, qb * 128:(qb + 1) * 128],
                                       k_sb[bp:bp + 64, hp, kt * 512:(kt + 1) * 512],
                                       start=True, stop=True)
                    tsc = C.inc(mm, "pe")
                    C.waits(ve, [tsc, nm_free])
                    v = ve.reduce_max(nm_s[:], ps_s[:], axis=AX.X, negate=True)
                    tnm = C.inc(v, "dv")
                    drive_prep(1)
                    C.waits(ae, [tnm, P_free, rs_free])
                    a = ae.activation(P_sb[:], ps_s[:], AF.Exp, bias=nm_s[:], scale=1.0, accum_out=rs_s[:])
                    texp = C.inc(a, "ac")
                    ps_s_free = [texp, tnm]
                    C.waits(ve, [texp, ri_free])
                    v = ve.tensor_scalar(nm_s[:], rs_s[:], 512.0 ** 0.5, None, op0=OP.mult)
                    trs2 = C.inc(v, "dv")
                    C.wait(ve, trs2)
                    v = ve.reciprocal(ri_s[:], nm_s[:])
                    tri = C.inc(v, "dv")
                    rs_free = tri
                    nm_free = tri
                    C.waits(ae, [tri, P2_free])
                    a = ae.activation(P2_sb[:], P_sb[:], AF.Copy, scale=ri_s[:])
                    tp2 = C.inc(a, "ac")
                    P_free = tp2
                    ri_free = tp2
                    C.waits(pe, [tp2, tconst])
                    tevac = None
                    tlast_tp = None
                    for j in range(16):
                        sl = (j // 4) % 2
                        if j % 4 == 0:
                            C.wait(pe, ring_free[sl])
                        mm = pe.transpose(ps_tr[:, sl, (j % 4) * 128:(j % 4 + 1) * 128],
                                          P2_sb[:, j * 128:(j + 1) * 128], ident[:])
                        if j % 4 == 3:
                            tlast_tp = C.inc(mm, "pe")
                            C.waits(ae, [tlast_tp, PT_free])
                            a = ae.activation(PT_sb[:, j - 3:j + 1, :], ps_tr[:, sl, :], AF.Copy)
                            tevac = C.inc(a, "ac")
                            ring_free[sl] = tevac
                    P2_free = tlast_tp
                    sl = it % 2
                    C.waits(pe, [tevac, ps_pv_free[sl]])
                    for tc in range(16):
                        mm = pe.matmul(ps_pv[bp:bp + 64, sl, 0:128], v_sb[:, tc, h * 64:(h + 1) * 64],
                                       PT_sb[:, tc, :], start=(tc == 0), stop=(tc == 15),
                                       tile_position=(0, bp))
                    tpv = C.inc(mm, "pe")
                    PT_free = tpv
                    C.wait(ve, tpv)
                    v = ve.tensor_copy(pvt[bp:bp + 64, hp, qb * 128:(qb + 1) * 128],
                                       ps_pv[bp:bp + 64, sl, 0:128])
                    ps_pv_free[sl] = C.inc(v, "dv")
                    it += 1
                    drive_prep(2)
            barrier()

        with ExitStack() as es3:
            sb3 = lambda n, s, ty: es3.enter_context(nc.sbuf_tensor(n, s, ty))
            wo_b = sb3("wo_b", [128, 2, 4, 128], F32)
            xr_b = sb3("xr_b", [128, 2, 512], F32)
            ps_o = es3.enter_context(nc.psum_tensor("ps_o", [128, 2, 512], F32))

            wo_tok = [None, None]
            wo_free = [None, None]
            xr_tok = [None, None]
            xr_free = [None, None]
            ps_o_free = [None, None]
            stg_free = [None, None]
            tpvt_all = C.tok("dv")
            wot_r = wot.rearrange("(ec p) d -> p ec d", p=128)

            def load_wo_xr(idx):
                tb2, dc = idx // 32, idx % 32
                sl = idx % 2
                wo_tok[sl] = C.dma(se, f"wo{sl}", wo_b[:, sl, :, :],
                                   wot_r[:, :, dc * 128:(dc + 1) * 128], [wo_free[sl]])
                xr_tok[sl] = C.dma(ge, f"xr{sl}", xr_b[:, sl, :],
                                   xt[dc * 128:(dc + 1) * 128, tb2 * 512:(tb2 + 1) * 512],
                                   [xr_free[sl]])

            load_wo_xr(0)
            load_wo_xr(1)
            for idx in range(64):
                tb2, dc = idx // 32, idx % 32
                sl = idx % 2
                C.waits(pe, [wo_tok[sl], tpvt_all, ps_o_free[sl]])
                for ec in range(4):
                    mm = pe.matmul(ps_o[:, sl, :], wo_b[:, sl, ec, :],
                                   pvt[:, ec, tb2 * 512:(tb2 + 1) * 512],
                                   start=(ec == 0), stop=(ec == 3))
                tmm = C.inc(mm, "pe")
                wo_free[sl] = tmm
                C.waits(ve, [tmm, xr_tok[sl], tconst, stg_free[sl]])
                v = ve.scalar_tensor_tensor(stg[:, sl, :], ps_o[:, sl, :], bo_s[:, dc:dc + 1],
                                            xr_b[:, sl, :], op0=OP.add, op1=OP.add)
                tev = C.inc(v, "dv")
                ps_o_free[sl] = tev
                xr_free[sl] = tev
                stg_free[sl] = C.dma(ge, f"oy{sl}",
                                     yt[dc * 128:(dc + 1) * 128, tb2 * 512:(tb2 + 1) * 512],
                                     stg[:, sl, :], [tev])
                if idx + 2 < 64:
                    load_wo_xr(idx + 2)
                drive_prep(2)
            barrier()
        esA.close()
        ty_all = [C.tok("oy0"), C.tok("oy1")]

        with ExitStack() as esB:
            sbB = lambda n, s, ty: esB.enter_context(nc.sbuf_tensor(n, s, ty))
            xh = sbB("xh", [128, DC, HALF], BF16)
            xl = sbB("xl", [128, DC, HALF], BF16)
            h_hi = sbB("h_hi", [128, 11, HALF], BF16)
            h_lo = sbB("h_lo", [128, 11, HALF], BF16)
            scr2 = sbB("scr2", [128, 2, HALF], F32)
            sgb = sbB("sgb", [128, HALF], F32)
            htmp = sbB("htmp", [128, HALF], F32)
            dwq = sbB("dwq", [128, 2816], I32)
            dzb = sbB("dzb", [128, 2, 86], F32)
            dsb = sbB("dsb", [128, 2, 86], F32)
            dwhiT = sbB("dwhiT", [128, 2, 11, 128], BF16)
            dwloT = sbB("dwloT", [128, 2, 11, 128], BF16)
            plod = sbB("plod", [128, 2, 512], F32)

            xh_tok = [None] * DC
            xl_tok = [None] * DC
            xhl_free = [None] * DC
            h_tok = [None] * 11
            h_free = [None] * 11
            dwq_tok = None
            dwq_free = None
            dzs_tok = [None, None]
            dzs_free = [None, None]
            dwT_tok = [None, None]
            dwT_free = [None, None]
            plod_free = [None, None]
            stgB_free = [None, None]
            scr2_free = [None, None]
            sgb_free = None
            htmp_free = None
            py_tok = {}
            sb_start = np.cumsum([0] + SBS)

            for half in range(2):
                hc0 = half * HALF
                sp_all = [C.tok(s) for s in ("sp0", "sp1", "spd0", "spd1")]
                with ExitStack() as es0:
                    ps_st2 = es0.enter_context(nc.psum_tensor(f"ps_st2_{half}", [1, 1024], F32))
                    ps_b2 = es0.enter_context(nc.psum_tensor(f"ps_b2_{half}", [128, 1024], F32))
                    for dc in range(DC):
                        sl = dc % 2
                        tx2 = C.dma(ge, f"x2{sl}", scr2[:, sl, :],
                                    yt[dc * 128:(dc + 1) * 128, hc0:hc0 + HALF],
                                    [ty_all, scr2_free[sl]])
                        C.waits(ae, [tx2, sgb_free])
                        a = ae.activation(sgb[:], scr2[:, sl, :], AF.Square)
                        tsq = C.inc(a, "ac")
                        C.waits(pe, [tx2, tsq, tconst])
                        pe.matmul(ps_st2[0:1, 0:HALF], onesc[:], scr2[:, sl, :],
                                  start=(dc == 0), stop=(dc == DC - 1))
                        m2 = pe.matmul(ps_st2[0:1, HALF:2 * HALF], onesc[:], sgb[:],
                                       start=(dc == 0), stop=(dc == DC - 1))
                        t = C.inc(m2, "pe")
                        sgb_free = t
                        scr2_free[sl] = [t, tsq]
                        if half == 0:
                            drive_prep(1)
                    tstat = C.tok("pe")
                    C.wait(ve, tstat)
                    v = ve.tensor_scalar(strow[0:1, 0:512], ps_st2[0:1, 0:512], 1.0 / D, None, op0=OP.mult)
                    C.inc(v, "dv")
                    v = ve.tensor_scalar(strow[0:1, 512:1024], ps_st2[0:1, 512:1024], 1.0 / D,
                                         None, op0=OP.mult)
                    tv = C.inc(v, "dv")
                    C.wait(ve, tv)
                    v = ve.tensor_tensor(strow[0:1, 1024:1536], strow[0:1, 0:512], strow[0:1, 0:512],
                                         op=OP.mult)
                    tv = C.inc(v, "dv")
                    C.wait(ve, tv)
                    v = ve.scalar_tensor_tensor(strow[0:1, 1536:2048], strow[0:1, 512:1024],
                                                EPS, strow[0:1, 1024:1536],
                                                op0=OP.add, op1=OP.subtract)
                    tv = C.inc(v, "dv")
                    C.wait(ae, tv)
                    a = ae.activation(strow[0:1, 1024:1536], strow[0:1, 1536:2048], AF.Sqrt)
                    ta = C.inc(a, "ac")
                    C.wait(ve, ta)
                    v = ve.reciprocal(strow[0:1, 512:1024], strow[0:1, 1024:1536])
                    trstd = C.inc(v, "dv")
                    C.waits(pe, [trstd, tconst])
                    pe.matmul(ps_b2[:, 0:512], onesr[:], strow[0:1, 0:512], start=True, stop=True)
                    mb = pe.matmul(ps_b2[:, 512:1024], onesr[:], strow[0:1, 512:1024],
                                   start=True, stop=True)
                    tbb = C.inc(mb, "pe")
                    C.wait(ve, tbb)
                    v = ve.tensor_copy(mrrep[:, 0:1024], ps_b2[:, 0:1024])
                    trep = C.inc(v, "dv")
                    for dc in range(DC):
                        sl = dc % 2
                        tx2 = C.dma(ge, f"x2{sl}", scr2[:, sl, :],
                                    yt[dc * 128:(dc + 1) * 128, hc0:hc0 + HALF],
                                    [scr2_free[sl]])
                        C.waits(ve, [tx2, trep, htmp_free])
                        v = ve.tensor_tensor(htmp[:], scr2[:, sl, :], mrrep[:, 0:512], op=OP.subtract)
                        tv = C.inc(v, "dv")
                        scr2_free[sl] = [tv]
                        C.waits(ve, [tv, sgb_free])
                        v = ve.tensor_tensor(sgb[:], htmp[:], mrrep[:, 512:1024], op=OP.mult)
                        tln = C.inc(v, "dv")
                        htmp_free = tln
                        C.waits(ae, [tln, xhl_free[dc]])
                        a = ae.activation(xh[:, dc, :], sgb[:], AF.Copy)
                        thi = C.inc(a, "ac")
                        xh_tok[dc] = thi
                        C.waits(ve, [thi])
                        v = ve.tensor_tensor(xl[:, dc, :], sgb[:], xh[:, dc, :], op=OP.subtract)
                        tlo = C.inc(v, "dv")
                        xl_tok[dc] = tlo
                        sgb_free = [thi, tlo]

                with ExitStack() as es12:
                    ps_g = es12.enter_context(nc.psum_tensor(f"ps_g_{half}", [128, 2, 512], F32))
                    ps_u = es12.enter_context(nc.psum_tensor(f"ps_u_{half}", [128, 2, 512], F32))
                    ps_d = es12.enter_context(nc.psum_tensor(f"ps_d_{half}", [128, 2, 512], F32))
                    ps_gu_free = [None, None]
                    ps_d_free = [None, None]
                    g_b1start = [prep_done[0]]

                    for sbi in range(NSB):
                        i0 = int(sb_start[sbi])
                        sbn = SBS[sbi]
                        for icn in range(sbn):
                            ic = i0 + icn
                            tg = tu = None
                            for wi in range(2):
                                dsl = wi
                                wsp_w = wsp_g if wi == 0 else wsp_u
                                s_idx = 2 * ic + wi
                                if half == 0 and s_idx >= g_b1start[0]:
                                    while prep_done[0] <= s_idx:
                                        drive_prep(1)
                                    sp_tok = [sp_done[s_idx]]
                                else:
                                    th_rl = C.dma(se, f"rl{dsl}", whiT[:, dsl, :, :], wsp_w[ic, 0],
                                                  [wT_free[dsl], sp_done[s_idx]])
                                    tl_rl = C.dma(se, f"rl{dsl}", wloT[:, dsl, :, :], wsp_w[ic, 1], [])
                                    wT_tok[dsl] = [[tl_rl]] * 8
                                    sp_tok = []
                                psd = ps_g if wi == 0 else ps_u
                                gsl = ic % 2
                                C.wait(pe, ps_gu_free[gsl])
                                for k in range(DC):
                                    C.waits(pe, [xh_tok[k], xl_tok[k], wT_tok[dsl][k // 4]])
                                    pe.matmul(psd[:, gsl, :], whiT[:, dsl, k, :], xh[:, k, :],
                                              start=(k == 0), stop=False)
                                    pe.matmul(psd[:, gsl, :], whiT[:, dsl, k, :], xl[:, k, :],
                                              start=False, stop=False)
                                    mm = pe.matmul(psd[:, gsl, :], wloT[:, dsl, k, :], xh[:, k, :],
                                                   start=False, stop=(k == DC - 1))
                                tmm = C.inc(mm, "pe")
                                wT_free[dsl] = [tmm] + wT_tok[dsl][-1] + sp_tok
                                if wi == 0:
                                    tg = tmm
                                else:
                                    tu = tmm
                            gsl = ic % 2
                            C.waits(ae, [tg, sgb_free])
                            a = ae.activation(sgb[:], ps_g[:, gsl, :], AF.Silu)
                            tsg = C.inc(a, "ac")
                            C.waits(ve, [tsg, tu, htmp_free])
                            v = ve.tensor_tensor(htmp[:], sgb[:], ps_u[:, gsl, :], op=OP.mult)
                            th = C.inc(v, "dv")
                            sgb_free = th
                            ps_gu_free[gsl] = [tsg, th]
                            C.waits(ae, [th, h_free[icn]])
                            a = ae.activation(h_hi[:, icn, :], htmp[:], AF.Copy)
                            thi = C.inc(a, "ac")
                            C.waits(ve, [thi, h_free[icn]])
                            v = ve.tensor_tensor(h_lo[:, icn, :], htmp[:], h_hi[:, icn, :], op=OP.subtract)
                            tlo = C.inc(v, "dv")
                            htmp_free = [thi, tlo]
                            h_tok[icn] = [thi, tlo]
                        t_sb_last_mm = None
                        for dc in range(DC):
                            dsl = dc % 2
                            if half == 1:
                                th_rl = C.dma(se, f"rld{dsl}", dwhiT[:, dsl, 0:sbn, :],
                                              wspd[sbi, dc, 0, :, 0:sbn * 128],
                                              [dwT_free[dsl], sp_all])
                                tl_rl = C.dma(se, f"rld{dsl}", dwloT[:, dsl, 0:sbn, :],
                                              wspd[sbi, dc, 1, :, 0:sbn * 128], [])
                                dwT_tok[dsl] = [[tl_rl]] * 3
                                dsp_tok = []
                            else:
                                dwq_tok = C.dma(se, "dwq", dwq[:, 0:sbn * 128],
                                                dqw[dc * 128:(dc + 1) * 128, i0 * 128:(i0 + sbn) * 128],
                                                [dwq_free])
                                zsl = dc % 2
                                C.dma(se, f"dzs{zsl}", dzb[:, zsl, :],
                                      dzt[dc * 128:(dc + 1) * 128, :], [dzs_free[zsl]])
                                dzs_tok[zsl] = C.dma(se, f"dzs{zsl}", dsb[:, zsl, :],
                                                     dst_[dc * 128:(dc + 1) * 128, :], [])
                            dbatches = []
                            old_free = dwT_free[dsl]
                            tdq_last = None
                            ttps = []

                            def devac(b):
                                psl = b % 2
                                b0 = b * 4
                                nb = min(4, sbn - b0)
                                C.waits(ae, [ttps[b], old_free])
                                a = ae.activation(dwhiT[:, dsl, b0:b0 + nb, :],
                                                  ps_tr[:, psl, 0:nb * 128], AF.Copy)
                                thi = C.inc(a, "ac")
                                C.waits(ve, [thi, ttps[b], old_free])
                                v = ve.tensor_tensor(dwloT[:, dsl, b0:b0 + nb, :],
                                                     ps_tr[:, psl, 0:nb * 128],
                                                     dwhiT[:, dsl, b0:b0 + nb, :], op=OP.subtract)
                                tlo = C.inc(v, "dv")
                                dbatches.append([thi, tlo])
                                ring_free[psl] = [thi, tlo]

                            if half == 0:
                                for j in range(sbn):
                                    wsl = j % 4
                                    C.waits(ve, [dwq_tok, dzs_tok[zsl], wtile_free[wsl]])
                                    v = ve.tensor_scalar(wtile[:, wsl, :], dwq[:, j * 128:(j + 1) * 128],
                                                         dzb[:, zsl, i0 + j:i0 + j + 1],
                                                         dsb[:, zsl, i0 + j:i0 + j + 1],
                                                         op0=OP.subtract, op1=OP.mult)
                                    tdq = C.inc(v, "dv")
                                    tdq_last = tdq
                                    psl = (j // 4) % 2
                                    if j % 4 == 0:
                                        C.wait(pe, ring_free[psl])
                                    C.waits(pe, [tdq, tconst])
                                    mm = pe.transpose(ps_tr[:, psl, (j % 4) * 128:(j % 4 + 1) * 128],
                                                      wtile[:, wsl, :], ident[:])
                                    ttp = C.inc(mm, "pe")
                                    wtile_free[wsl] = ttp
                                    if j % 4 == 3 or j == sbn - 1:
                                        ttps.append(ttp)
                                        if len(ttps) >= 2:
                                            devac(len(ttps) - 2)
                                devac(len(ttps) - 1)
                                dwq_free = tdq_last
                                dzs_free[zsl] = tdq_last
                                dwT_tok[dsl] = dbatches
                                th_sp = C.dma(ge, f"spd{dsl}", wspd[sbi, dc, 0, :, 0:sbn * 128],
                                              dwhiT[:, dsl, 0:sbn, :], [dbatches[-1][0]])
                                tl_sp = C.dma(ge, f"spd{dsl}", wspd[sbi, dc, 1, :, 0:sbn * 128],
                                              dwloT[:, dsl, 0:sbn, :], [dbatches[-1][1]])
                                dsp_tok = [tl_sp]
                            psl_d = dc % 2
                            C.wait(pe, ps_d_free[psl_d])
                            for icn in range(sbn):
                                C.waits(pe, [h_tok[icn], dwT_tok[dsl][icn // 4]])
                                pe.matmul(ps_d[:, psl_d, :], dwhiT[:, dsl, icn, :], h_hi[:, icn, :],
                                          start=(icn == 0), stop=False)
                                pe.matmul(ps_d[:, psl_d, :], dwhiT[:, dsl, icn, :], h_lo[:, icn, :],
                                          start=False, stop=False)
                                mm = pe.matmul(ps_d[:, psl_d, :], dwloT[:, dsl, icn, :], h_hi[:, icn, :],
                                               start=False, stop=(icn == sbn - 1))
                            tmm = C.inc(mm, "pe")
                            dwT_free[dsl] = [tmm] + dwT_tok[dsl][-1] + dsp_tok
                            t_sb_last_mm = tmm
                            if sbi < NSB - 1:
                                ssl = dc % 2
                                C.waits(ve, [tmm, stgB_free[ssl]])
                                v = ve.tensor_copy(stg[:, ssl, :], ps_d[:, psl_d, :])
                                tev = C.inc(v, "dv")
                                ps_d_free[psl_d] = tev
                                tst = C.dma(ge, f"op{ssl}",
                                            pscr[half, sbi, dc * 128:(dc + 1) * 128, :],
                                            stg[:, ssl, :], [tev])
                                stgB_free[ssl] = tst
                                py_tok[(half, sbi, dc)] = tst
                            else:
                                tx2 = C.dma(ge, "pl0", plod[:, 0, :],
                                            yt[dc * 128:(dc + 1) * 128, hc0:hc0 + HALF],
                                            [plod_free[0]])
                                s = 0
                                C.waits(ve, [tmm, tx2, scr2_free[0]])
                                v = ve.tensor_tensor(scr2[:, 0, :], ps_d[:, psl_d, :], plod[:, 0, :],
                                                     op=OP.add)
                                tacc = C.inc(v, "dv")
                                ps_d_free[psl_d] = tacc
                                plod_free[0] = tacc
                                scr2_read = [None, None]
                                tx2_tok = tx2
                                last_pl = [tx2_tok, None]
                                for pj in range(NSB - 1):
                                    lsl = (pj + 1) % 2
                                    tpl = C.dma(ge, f"pl{lsl}", plod[:, lsl, :],
                                                pscr[half, pj, dc * 128:(dc + 1) * 128, :],
                                                [plod_free[lsl], py_tok[(half, pj, dc)]])
                                    last_pl[lsl] = tpl
                                    ns = 1 - s
                                    C.waits(ve, [tpl, tacc, scr2_free[ns]])
                                    v = ve.tensor_tensor(scr2[:, ns, :], scr2[:, s, :], plod[:, lsl, :],
                                                         op=OP.add)
                                    tacc = C.inc(v, "dv")
                                    plod_free[lsl] = tacc
                                    scr2_read[s] = tacc
                                    s = ns
                                C.waits(ge, [tacc, last_pl[0], last_pl[1]])
                                tst = C.dma(ge, f"oy{s}",
                                            yt[dc * 128:(dc + 1) * 128, hc0:hc0 + HALF],
                                            scr2[:, s, :], [])
                                scr2_free[s] = [tst]
                                scr2_free[1 - s] = [scr2_read[1 - s]]
                        for icn in range(11):
                            if h_tok[icn] is not None:
                                h_free[icn] = [t_sb_last_mm] + h_tok[icn][1:]
                barrier()
                for dc in range(DC):
                    xhl_free[dc] = C.tok("pe")
            for sname in ("oy0", "oy1", "op0", "op1"):
                C.wait(ge, C.tok(sname))
    return nc


def _host_prep(inputs):
    x = np.asarray(inputs["L0_attn_x"], dtype=np.float32)
    wqk = np.asarray(inputs["L0_attn_Wqk_w"], dtype=np.float32)
    bqk = np.asarray(inputs["L0_attn_Wqk_b"], dtype=np.float32)
    wv = np.asarray(inputs["L0_attn_Wv_w"], dtype=np.float32)
    bv = np.asarray(inputs["L0_attn_Wv_b"], dtype=np.float32)
    wo = np.asarray(inputs["L0_attn_Wo_w"], dtype=np.float32)
    bo = np.asarray(inputs["L0_attn_Wo_b"], dtype=np.float32)

    perm = np.zeros(1024, dtype=np.int64)
    e = np.arange(64)
    for hp in range(4):
        h0, h1 = 2 * hp, 2 * hp + 1
        base = hp * 256
        perm[base + 0:base + 64] = h0 * 128 + 2 * e
        perm[base + 64:base + 128] = h1 * 128 + 2 * e
        perm[base + 128:base + 192] = h0 * 128 + 2 * e + 1
        perm[base + 192:base + 256] = h1 * 128 + 2 * e + 1

    common = dict(
        wqkt=np.ascontiguousarray(wqk[perm].T),
        bqk=np.ascontiguousarray(bqk[perm].reshape(8, 128).T),
        wvt=np.ascontiguousarray(wv.T),
        bvr=np.ascontiguousarray(np.broadcast_to(bv[None, :], (128, 512))),
        wot=np.ascontiguousarray(wo.T),
        bo=np.ascontiguousarray(bo.reshape(32, 128).T),
        gq=np.ascontiguousarray(inputs["L0_ffn_gate_qweight"], dtype=np.int32),
        gz=np.ascontiguousarray(inputs["L0_ffn_gate_zeros"], dtype=np.float32),
        gs=np.ascontiguousarray(inputs["L0_ffn_gate_scales"], dtype=np.float32),
        uq=np.ascontiguousarray(inputs["L0_ffn_up_qweight"], dtype=np.int32),
        uz=np.ascontiguousarray(inputs["L0_ffn_up_zeros"], dtype=np.float32),
        us=np.ascontiguousarray(inputs["L0_ffn_up_scales"], dtype=np.float32),
        dqw=np.ascontiguousarray(inputs["L0_ffn_down_qweight"], dtype=np.int32),
        dzt=np.ascontiguousarray(inputs["L0_ffn_down_zeros"], dtype=np.float32),
        dst=np.ascontiguousarray(inputs["L0_ffn_down_scales"], dtype=np.float32),
        identd=np.eye(128, dtype=np.float32),
        onescd=np.ones((128, 1), dtype=np.float32),
        onesrd=np.ones((1, 128), dtype=np.float32),
    )

    in_maps = []
    for c in range(8):
        b, hf = c // 2, c % 2
        xb = x[b].T
        own = xb[:, hf * 1024:(hf + 1) * 1024]
        other = xb[:, (1 - hf) * 1024:(2 - hf) * 1024]
        im = dict(common)
        im["xt"] = np.ascontiguousarray(np.concatenate([own, other], axis=1))
        in_maps.append(im)
    return in_maps


def kernel(**inputs):
    in_maps = _host_prep(inputs)
    nc = build_nc()
    res = run_bass_kernel_spmd(nc, in_maps, core_ids=list(range(8)))
    out = np.empty((4, 2048, 4096), dtype=np.float32)
    for c in range(8):
        b, hf = c // 2, c % 2
        out[b, hf * 1024:(hf + 1) * 1024, :] = res.results[c]["yt"].T
    return out


# version 06
# speedup vs baseline: 1.0634x; 1.0634x over previous
import numpy as np
from contextlib import ExitStack
import concourse.bass as bass
import concourse.mybir as mybir
from concourse.alu_op_type import AluOpType as OP
from concourse.bass_utils import run_bass_kernel_spmd

F32 = mybir.dt.float32
BF16 = mybir.dt.bfloat16
I32 = mybir.dt.int32
AF = mybir.ActivationFunctionType
AX = mybir.AxisListType

D = 4096
DC = 32
SEQ = 2048
OWN = 1024
IF = 11008
ICN = 86
H = 8
TBA = 256
NTB = SEQ // TBA
HALF = 512
SBS = [11, 11, 11, 11, 11, 11, 11, 9]
NSB = len(SBS)
INV_SQRT_HD = 1.0 / (512.0 ** 0.5)
EPS = 1e-5


class Ctl:

    def __init__(self, nc, es):
        self.nc = nc
        self.sem = {}
        self.count = {}
        self.hw = {}
        self.es = es

    def new(self, *names):
        for n in names:
            self.sem[n] = self.es.enter_context(self.nc.semaphore(n))
            self.count[n] = 0

    def inc(self, instr, name, n=1):
        self.count[name] += n
        instr.then_inc(self.sem[name], n)
        return (name, self.count[name])

    def tok(self, name):
        return (name, self.count[name])

    def wait(self, eng, tok):
        if tok is None:
            return
        if isinstance(tok, (list, tuple)) and tok and isinstance(tok[0], (list, tuple, type(None))):
            for t in tok:
                self.wait(eng, t)
            return
        name, val = tok
        if val <= 0:
            return
        key = (id(eng), name)
        if self.hw.get(key, 0) >= val:
            return
        eng.wait_ge(self.sem[name], val)
        self.hw[key] = val

    def waits(self, eng, toks):
        for t in toks:
            self.wait(eng, t)

    def dma(self, eng, sem, out, in_, frees=()):
        for f in frees:
            self.wait(eng, f)
        d = eng.dma_start(out=out, in_=in_)
        return self.inc(d, sem, 16)


def build_nc():
    nc = bass.Bass("TRN2", target_bir_lowering=False)
    dram = lambda n, s, ty, k: nc.dram_tensor(n, s, ty, kind=k).ap()
    xt = dram("xt", [D, SEQ], F32, "ExternalInput")
    wqkt = dram("wqkt", [D, 1024], F32, "ExternalInput")
    bqk = dram("bqk", [128, 8], F32, "ExternalInput")
    wvt = dram("wvt", [D, 512], F32, "ExternalInput")
    bvr = dram("bvr", [128, 512], F32, "ExternalInput")
    wot = dram("wot", [512, D], F32, "ExternalInput")
    bo = dram("bo", [128, 32], F32, "ExternalInput")
    gq = dram("gq", [IF, D], I32, "ExternalInput")
    gz = dram("gz", [IF, 32], F32, "ExternalInput")
    gs = dram("gs", [IF, 32], F32, "ExternalInput")
    uq = dram("uq", [IF, D], I32, "ExternalInput")
    uz = dram("uz", [IF, 32], F32, "ExternalInput")
    us = dram("us", [IF, 32], F32, "ExternalInput")
    dqw = dram("dqw", [D, IF], I32, "ExternalInput")
    dzt = dram("dzt", [D, 86], F32, "ExternalInput")
    dst_ = dram("dst", [D, 86], F32, "ExternalInput")
    identd = dram("identd", [128, 128], F32, "ExternalInput")
    onescd = dram("onescd", [128, 1], F32, "ExternalInput")
    onesrd = dram("onesrd", [1, 128], F32, "ExternalInput")
    yt = dram("yt", [D, OWN], F32, "ExternalOutput")
    pscr = dram("pscr", [2, NSB - 1, D, HALF], F32, "Internal")
    wsp_g = dram("wsp_g", [ICN, 2, 128, DC * 128], BF16, "Internal")
    wsp_u = dram("wsp_u", [ICN, 2, 128, DC * 128], BF16, "Internal")
    wspd = dram("wspd", [NSB, DC, 2, 128, 11 * 128], BF16, "Internal")

    with ExitStack() as es:
        C = Ctl(nc, es)
        C.new("pe", "dv", "ac", "bar", "d_const",
              "wqk0", "wqk1", "wv0", "wv1", "wo0", "wo1", "xr0", "xr1",
              "x20", "x21", "wqA0", "wqA1", "zs0", "zs1", "dwq",
              "dzs0", "dzs1", "pl0", "pl1", "oy0", "oy1", "op0", "op1",
              "sp0", "sp1", "spd0", "spd1", "rl0", "rl1", "rld0", "rld1")
        C.new(*[f"xb{i}" for i in range(DC)])

        pe, ve, ae, ge, se = nc.tensor, nc.vector, nc.scalar, nc.gpsimd, nc.sync
        ENGS = [pe, ve, ae, ge, se]
        sbuf = lambda n, s, ty: es.enter_context(nc.sbuf_tensor(n, s, ty))

        ident = sbuf("ident", [128, 128], F32)
        onesc = sbuf("onesc", [128, 1], F32)
        onesr = sbuf("onesr", [1, 128], F32)
        bqk_s = sbuf("bqk_s", [128, 8], F32)
        bvr_s = sbuf("bvr_s", [128, 512], F32)
        bo_s = sbuf("bo_s", [128, 32], F32)
        strow = sbuf("strow", [1, 2048], F32)
        mrrep = sbuf("mrrep", [128, 1024], F32)
        nm_s = sbuf("nm_s", [128, 1], F32)
        rs_s = sbuf("rs_s", [128, 1], F32)
        ri_s = sbuf("ri_s", [128, 1], F32)
        stg = sbuf("stg", [128, 2, 512], F32)

        def barrier():
            names = [n for n in C.count if C.count[n] > 0]
            for e in ENGS:
                for n in names:
                    C.wait(e, (n, C.count[n]))

        for dst_t, src in ((ident, identd), (onesc, onescd), (onesr, onesrd),
                           (bqk_s, bqk), (bvr_s, bvr), (bo_s, bo)):
            d = se.dma_start(out=dst_t[:], in_=src)
            C.inc(d, "d_const", 16)
        tconst = C.tok("d_const")

        esA = ExitStack()
        sbA = lambda n, s, ty: esA.enter_context(nc.sbuf_tensor(n, s, ty))
        q_sb = sbA("q_sb", [128, 4, 1024], F32)
        k_sb = sbA("k_sb", [128, 4, 2048], F32)
        v_sb = sbA("v_sb", [128, 16, 512], F32)
        pvt = sbA("pvt", [128, 4, 1024], F32)

        with ExitStack() as es1:
            sb1 = lambda n, s, ty: es1.enter_context(nc.sbuf_tensor(n, s, ty))
            xblk = sb1("xblk", [128, DC, TBA], F32)
            xln = sb1("xln", [128, DC, TBA], F32)
            sqb = sb1("sqb", [128, 2, TBA], F32)
            tmpb = sb1("tmpb", [128, 2, TBA], F32)
            wqk_b = sb1("wqk_b", [128, 2, 512], F32)
            wv_b = sb1("wv_b", [128, 2, 512], F32)
            ps_stat = es1.enter_context(nc.psum_tensor("ps_stat", [1, 768], F32))
            ps_qk = es1.enter_context(nc.psum_tensor("ps_qk", [128, 4, 512], F32))
            ps_v = es1.enter_context(nc.psum_tensor("ps_v", [128, 2, 512], F32))

            xblk_free = [None] * DC
            xln_free = [None] * DC
            xln_tok = [None] * DC
            sq_free = [None, None]
            tmp_free = [None, None]
            wqk_tok = [None, None]
            wqk_free = [None, None]
            wv_tok = [None, None]
            wv_free = [None, None]
            ps_stat_free = None
            ps_qk_free = None
            ps_v_free = None

            wqk_seq = [(t_, p_, k_) for t_ in range(NTB) for p_ in range(2) for k_ in range(DC)]
            wqk_ptr = [0]

            def load_wqk_next():
                if wqk_ptr[0] >= len(wqk_seq):
                    return
                t_, p_, k_ = wqk_seq[wqk_ptr[0]]
                sl = wqk_ptr[0] % 2
                wqk_ptr[0] += 1
                wqk_tok[sl] = C.dma(se, f"wqk{sl}", wqk_b[:, sl, :],
                                    wqkt[k_ * 128:(k_ + 1) * 128, p_ * 512:(p_ + 1) * 512],
                                    [wqk_free[sl]])

            wv_seq = [(t_, k_) for t_ in range(NTB) for k_ in range(DC)]
            wv_ptr = [0]

            def load_wv_next():
                if wv_ptr[0] >= len(wv_seq):
                    return
                t_, k_ = wv_seq[wv_ptr[0]]
                sl = wv_ptr[0] % 2
                wv_ptr[0] += 1
                wv_tok[sl] = C.dma(se, f"wv{sl}", wv_b[:, sl, :],
                                   wvt[k_ * 128:(k_ + 1) * 128, :], [wv_free[sl]])

            load_wqk_next()
            load_wqk_next()
            load_wv_next()
            load_wv_next()
            wqk_use = [0]
            wv_use = [0]

            for tb in range(NTB):
                xblk_tok = [None] * DC
                for dc in range(DC):
                    xblk_tok[dc] = C.dma(ge, f"xb{dc}", xblk[:, dc, :],
                                         xt[dc * 128:(dc + 1) * 128, tb * TBA:(tb + 1) * TBA],
                                         [xblk_free[dc]])
                C.wait(pe, ps_stat_free)
                for dc in range(DC):
                    sl = dc % 2
                    C.waits(ae, [xblk_tok[dc], sq_free[sl]])
                    a = ae.activation(sqb[:, sl, :], xblk[:, dc, :], AF.Square)
                    tsq = C.inc(a, "ac")
                    C.waits(pe, [xblk_tok[dc], tsq, tconst])
                    pe.matmul(ps_stat[0:1, 0:TBA], onesc[:], xblk[:, dc, :],
                              start=(dc == 0), stop=(dc == DC - 1))
                    m2 = pe.matmul(ps_stat[0:1, 512:512 + TBA], onesc[:], sqb[:, sl, :],
                                   start=(dc == 0), stop=(dc == DC - 1))
                    t = C.inc(m2, "pe")
                    sq_free[sl] = t
                    xblk_free[dc] = [t, tsq]
                tstat = C.tok("pe")
                C.wait(ve, tstat)
                v = ve.tensor_scalar(strow[0:1, 0:TBA], ps_stat[0:1, 0:TBA], 1.0 / D, None, op0=OP.mult)
                C.inc(v, "dv")
                v = ve.tensor_scalar(strow[0:1, 256:256 + TBA], ps_stat[0:1, 512:512 + TBA], 1.0 / D,
                                     None, op0=OP.mult)
                tv = C.inc(v, "dv")
                ps_stat_free = tv
                C.wait(ve, tv)
                v = ve.tensor_tensor(strow[0:1, 512:512 + TBA], strow[0:1, 0:TBA], strow[0:1, 0:TBA],
                                     op=OP.mult)
                tv = C.inc(v, "dv")
                C.wait(ve, tv)
                v = ve.scalar_tensor_tensor(strow[0:1, 768:768 + TBA], strow[0:1, 256:256 + TBA],
                                            EPS, strow[0:1, 512:512 + TBA],
                                            op0=OP.add, op1=OP.subtract)
                tv = C.inc(v, "dv")
                C.wait(ae, tv)
                a = ae.activation(strow[0:1, 1024:1024 + TBA], strow[0:1, 768:768 + TBA], AF.Sqrt)
                ta = C.inc(a, "ac")
                C.wait(ve, ta)
                v = ve.reciprocal(strow[0:1, 1280:1280 + TBA], strow[0:1, 1024:1024 + TBA])
                trstd = C.inc(v, "dv")
                C.waits(pe, [trstd, tconst, ps_v_free])
                pe.matmul(ps_v[:, 0, 0:TBA], onesr[:], strow[0:1, 0:TBA], start=True, stop=True)
                mb = pe.matmul(ps_v[:, 0, 256:256 + TBA], onesr[:], strow[0:1, 1280:1280 + TBA],
                               start=True, stop=True)
                tbb = C.inc(mb, "pe")
                C.wait(ve, tbb)
                v = ve.tensor_copy(mrrep[:, 0:512], ps_v[:, 0, 0:512])
                trep = C.inc(v, "dv")
                tlast_k = None
                for k in range(DC):
                    sl = k % 2
                    C.waits(ve, [xblk_tok[k], trep, tmp_free[sl]])
                    v = ve.tensor_tensor(tmpb[:, sl, :], xblk[:, k, :], mrrep[:, 0:TBA], op=OP.subtract)
                    tv = C.inc(v, "dv")
                    xblk_free[k] = xblk_free[k] + [tv]
                    C.waits(ve, [tv, xln_free[k]])
                    v = ve.tensor_tensor(xln[:, k, :], tmpb[:, sl, :], mrrep[:, 256:256 + TBA], op=OP.mult)
                    txln = C.inc(v, "dv")
                    tmp_free[sl] = txln
                    xln_tok[k] = txln
                    wsl = wqk_use[0] % 2
                    wqk_use[0] += 1
                    vsl = wv_use[0] % 2
                    wv_use[0] += 1
                    C.waits(pe, [txln, wqk_tok[wsl], wv_tok[vsl], ps_qk_free, ps_v_free])
                    for m in range(4):
                        pe.matmul(ps_qk[:, m, 0:TBA], wqk_b[:, wsl, m * 128:(m + 1) * 128],
                                  xln[:, k, :], start=(k == 0), stop=(k == DC - 1))
                    for ts in range(2):
                        mm = pe.matmul(ps_v[:, ts, :], xln[:, k, ts * 128:(ts + 1) * 128],
                                       wv_b[:, vsl, :], start=(k == 0), stop=(k == DC - 1))
                    tlast_k = C.inc(mm, "pe")
                    wqk_free[wsl] = tlast_k
                    wv_free[vsl] = tlast_k
                    load_wqk_next()
                    load_wv_next()
                C.waits(ve, [tlast_k, tconst])
                for ts in range(2):
                    v = ve.tensor_tensor(v_sb[:, tb * 2 + ts, :], ps_v[:, ts, :], bvr_s[:], op=OP.add)
                    C.inc(v, "dv")
                ps_v_free = C.tok("dv")
                for m in range(4):
                    hp = m // 2
                    if m % 2 == 0:
                        if tb < 4:
                            v = ve.tensor_scalar(q_sb[:, hp, tb * TBA:(tb + 1) * TBA], ps_qk[:, m, 0:TBA],
                                                 bqk_s[:, m:m + 1], None, op0=OP.add)
                            C.inc(v, "dv")
                    else:
                        v = ve.tensor_scalar(k_sb[:, hp, tb * TBA:(tb + 1) * TBA], ps_qk[:, m, 0:TBA],
                                             bqk_s[:, m:m + 1], None, op0=OP.add)
                        C.inc(v, "dv")
                ps_qk_free = C.tok("dv")
                tlast_k = None
                for k in range(DC):
                    wsl = wqk_use[0] % 2
                    wqk_use[0] += 1
                    C.waits(pe, [xln_tok[k], wqk_tok[wsl], ps_qk_free])
                    for m in range(4):
                        mm = pe.matmul(ps_qk[:, m, 0:TBA], wqk_b[:, wsl, m * 128:(m + 1) * 128],
                                       xln[:, k, :], start=(k == 0), stop=(k == DC - 1))
                    tlast_k = C.inc(mm, "pe")
                    wqk_free[wsl] = tlast_k
                    xln_free[k] = tlast_k
                    load_wqk_next()
                C.waits(ve, [tlast_k, tconst])
                for m in range(4):
                    mg = m + 4
                    hp = mg // 2
                    if mg % 2 == 0:
                        if tb < 4:
                            v = ve.tensor_scalar(q_sb[:, hp, tb * TBA:(tb + 1) * TBA], ps_qk[:, m, 0:TBA],
                                                 bqk_s[:, mg:mg + 1], None, op0=OP.add)
                            C.inc(v, "dv")
                    else:
                        v = ve.tensor_scalar(k_sb[:, hp, tb * TBA:(tb + 1) * TBA], ps_qk[:, m, 0:TBA],
                                             bqk_s[:, mg:mg + 1], None, op0=OP.add)
                        C.inc(v, "dv")
                ps_qk_free = C.tok("dv")
            barrier()

        sbufR = lambda n, s, ty: es.enter_context(nc.sbuf_tensor(n, s, ty, side="right"))
        wq_b = sbufR("wq_b", [128, 2, 2048], I32)
        zb = sbufR("zb", [128, 2, 32], F32)
        szb = sbufR("szb", [128, 2, 32], F32)
        wtile = sbufR("wtile", [128, 4, 128], F32)
        whiT = sbufR("whiT", [128, 2, DC, 128], BF16)
        wloT = sbufR("wloT", [128, 2, DC, 128], BF16)
        ps_tr = es.enter_context(nc.psum_tensor("ps_tr", [128, 2, 512], F32))
        ring_free = [None, None]
        wq_tok = [None, None]
        wq_free = [None, None]
        zs_tok = [None, None]
        zs_free = [None, None]
        wtile_free = [None] * 4
        wT_tok = [None, None]
        wT_free = [None, None]
        sp_done = {}
        prep_done = [0]

        def strip_load(ic, wi):
            qsrc = gq if wi == 0 else uq
            zsrc = gz if wi == 0 else uz
            ssrc = gs if wi == 0 else us
            for hh in range(2):
                wq_tok[hh] = C.dma(se, f"wqA{hh}", wq_b[:, hh, :],
                                   qsrc[ic * 128:(ic + 1) * 128, hh * 2048:(hh + 1) * 2048],
                                   [wq_free[hh]])
            sl = wi
            C.dma(se, f"zs{sl}", zb[:, sl, :], zsrc[ic * 128:(ic + 1) * 128, :], [zs_free[sl]])
            zs_tok[sl] = C.dma(se, f"zs{sl}", szb[:, sl, :], ssrc[ic * 128:(ic + 1) * 128, :], [])

        def prep_strip_gen():
            strip_load(0, 0)
            for s in range(2 * ICN):
                ic, wi = s // 2, s % 2
                dsl = wi
                zsl = wi
                batches = []
                old_free = wT_free[dsl]
                ttps = []

                def evac(b):
                    psl = b % 2
                    j4 = b * 4
                    C.waits(ae, [ttps[b], old_free])
                    a = ae.activation(whiT[:, dsl, j4:j4 + 4, :], ps_tr[:, psl, :], AF.Copy)
                    thi = C.inc(a, "ac")
                    C.waits(ve, [thi, ttps[b], old_free])
                    v = ve.tensor_tensor(wloT[:, dsl, j4:j4 + 4, :], ps_tr[:, psl, :],
                                         whiT[:, dsl, j4:j4 + 4, :], op=OP.subtract)
                    tlo = C.inc(v, "dv")
                    batches.append([thi, tlo])
                    ring_free[psl] = [thi, tlo]

                for j in range(DC):
                    wsl = j % 4
                    hh = j // 16
                    C.waits(ve, [wq_tok[hh], zs_tok[zsl], wtile_free[wsl]])
                    v = ve.tensor_scalar(wtile[:, wsl, :], wq_b[:, hh, (j % 16) * 128:(j % 16 + 1) * 128],
                                         zb[:, zsl, j:j + 1], szb[:, zsl, j:j + 1],
                                         op0=OP.subtract, op1=OP.mult)
                    tdq = C.inc(v, "dv")
                    if j == 15:
                        wq_free[0] = tdq
                    if j == DC - 1:
                        wq_free[1] = tdq
                        zs_free[zsl] = tdq
                    psl = (j // 4) % 2
                    if j % 4 == 0:
                        C.wait(pe, ring_free[psl])
                    C.waits(pe, [tdq, tconst])
                    mm = pe.transpose(ps_tr[:, psl, (j % 4) * 128:(j % 4 + 1) * 128],
                                      wtile[:, wsl, :], ident[:])
                    ttp = C.inc(mm, "pe")
                    wtile_free[wsl] = ttp
                    if j % 4 == 3:
                        ttps.append(ttp)
                        if len(ttps) >= 2:
                            evac(len(ttps) - 2)
                evac(7)
                wT_tok[dsl] = batches
                nic, nwi = (ic, 1) if wi == 0 else (ic + 1, 0)
                if nic < ICN:
                    strip_load(nic, nwi)
                wsp_w = wsp_g if wi == 0 else wsp_u
                C.dma(ge, f"sp{dsl}", wsp_w[ic, 0], whiT[:, dsl, :, :], [batches[-1][0]])
                tl_sp = C.dma(ge, f"sp{dsl}", wsp_w[ic, 1], wloT[:, dsl, :, :], [batches[-1][1]])
                wT_free[dsl] = [tl_sp] + batches[-1]
                sp_done[s] = tl_sp
                prep_done[0] = s + 1
                yield

        prep_gen = prep_strip_gen()

        def drive_prep(n=1):
            for _ in range(n):
                if next(prep_gen, "END") == "END":
                    return

        with ExitStack() as es2:
            sb2 = lambda n, s, ty: es2.enter_context(nc.sbuf_tensor(n, s, ty))
            P_sb = sb2("P_sb", [128, 2048], F32)
            P2_sb = sb2("P2_sb", [128, 2048], F32)
            PT_sb = sb2("PT_sb", [128, 16, 128], F32)
            ps_s = es2.enter_context(nc.psum_tensor("ps_s", [128, 2048], F32))
            ps_pv = es2.enter_context(nc.psum_tensor("ps_pv", [128, 2, 512], F32))

            ps_s_free = None
            ps_pv_free = [None, None]
            P_free = None
            P2_free = None
            PT_free = None
            nm_free = None
            rs_free = None
            ri_free = None
            it = 0
            for h in range(H):
                bp = (h % 2) * 64
                hp = h // 2
                for qb in range(8):
                    C.wait(pe, ps_s_free)
                    for kt in range(4):
                        mm = pe.matmul(ps_s[:, kt * 512:(kt + 1) * 512],
                                       q_sb[bp:bp + 64, hp, qb * 128:(qb + 1) * 128],
                                       k_sb[bp:bp + 64, hp, kt * 512:(kt + 1) * 512],
                                       start=True, stop=True)
                    tsc = C.inc(mm, "pe")
                    C.waits(ve, [tsc, nm_free])
                    v = ve.reduce_max(nm_s[:], ps_s[:], axis=AX.X, negate=True)
                    tnm = C.inc(v, "dv")
                    drive_prep(1)
                    C.waits(ae, [tnm, P_free, rs_free])
                    a = ae.activation(P_sb[:], ps_s[:], AF.Exp, bias=nm_s[:], scale=1.0, accum_out=rs_s[:])
                    texp = C.inc(a, "ac")
                    ps_s_free = [texp, tnm]
                    C.waits(ve, [texp, ri_free])
                    v = ve.tensor_scalar(nm_s[:], rs_s[:], 512.0 ** 0.5, None, op0=OP.mult)
                    trs2 = C.inc(v, "dv")
                    C.wait(ve, trs2)
                    v = ve.reciprocal(ri_s[:], nm_s[:])
                    tri = C.inc(v, "dv")
                    rs_free = tri
                    nm_free = tri
                    C.waits(ae, [tri, P2_free])
                    a = ae.activation(P2_sb[:], P_sb[:], AF.Copy, scale=ri_s[:])
                    tp2 = C.inc(a, "ac")
                    P_free = tp2
                    ri_free = tp2
                    C.waits(pe, [tp2, tconst])
                    tevac = None
                    tlast_tp = None
                    for j in range(16):
                        sl = (j // 4) % 2
                        if j % 4 == 0:
                            C.wait(pe, ring_free[sl])
                        mm = pe.transpose(ps_tr[:, sl, (j % 4) * 128:(j % 4 + 1) * 128],
                                          P2_sb[:, j * 128:(j + 1) * 128], ident[:])
                        if j % 4 == 3:
                            tlast_tp = C.inc(mm, "pe")
                            C.waits(ae, [tlast_tp, PT_free])
                            a = ae.activation(PT_sb[:, j - 3:j + 1, :], ps_tr[:, sl, :], AF.Copy)
                            tevac = C.inc(a, "ac")
                            ring_free[sl] = tevac
                    P2_free = tlast_tp
                    sl = it % 2
                    C.waits(pe, [tevac, ps_pv_free[sl]])
                    for tc in range(16):
                        mm = pe.matmul(ps_pv[bp:bp + 64, sl, 0:128], v_sb[:, tc, h * 64:(h + 1) * 64],
                                       PT_sb[:, tc, :], start=(tc == 0), stop=(tc == 15),
                                       tile_position=(0, bp))
                    tpv = C.inc(mm, "pe")
                    PT_free = tpv
                    C.wait(ve, tpv)
                    v = ve.tensor_copy(pvt[bp:bp + 64, hp, qb * 128:(qb + 1) * 128],
                                       ps_pv[bp:bp + 64, sl, 0:128])
                    ps_pv_free[sl] = C.inc(v, "dv")
                    it += 1
            barrier()

        with ExitStack() as es3:
            sb3 = lambda n, s, ty: es3.enter_context(nc.sbuf_tensor(n, s, ty))
            wo_b = sb3("wo_b", [128, 2, 4, 128], F32)
            xr_b = sb3("xr_b", [128, 2, 512], F32)
            ps_o = es3.enter_context(nc.psum_tensor("ps_o", [128, 2, 512], F32))

            wo_tok = [None, None]
            wo_free = [None, None]
            xr_tok = [None, None]
            xr_free = [None, None]
            ps_o_free = [None, None]
            stg_free = [None, None]
            tpvt_all = C.tok("dv")
            wot_r = wot.rearrange("(ec p) d -> p ec d", p=128)

            def load_wo_xr(idx):
                tb2, dc = idx // 32, idx % 32
                sl = idx % 2
                wo_tok[sl] = C.dma(se, f"wo{sl}", wo_b[:, sl, :, :],
                                   wot_r[:, :, dc * 128:(dc + 1) * 128], [wo_free[sl]])
                xr_tok[sl] = C.dma(ge, f"xr{sl}", xr_b[:, sl, :],
                                   xt[dc * 128:(dc + 1) * 128, tb2 * 512:(tb2 + 1) * 512],
                                   [xr_free[sl]])

            load_wo_xr(0)
            load_wo_xr(1)
            for idx in range(64):
                tb2, dc = idx // 32, idx % 32
                sl = idx % 2
                C.waits(pe, [wo_tok[sl], tpvt_all, ps_o_free[sl]])
                for ec in range(4):
                    mm = pe.matmul(ps_o[:, sl, :], wo_b[:, sl, ec, :],
                                   pvt[:, ec, tb2 * 512:(tb2 + 1) * 512],
                                   start=(ec == 0), stop=(ec == 3))
                tmm = C.inc(mm, "pe")
                wo_free[sl] = tmm
                C.waits(ve, [tmm, xr_tok[sl], tconst, stg_free[sl]])
                v = ve.scalar_tensor_tensor(stg[:, sl, :], ps_o[:, sl, :], bo_s[:, dc:dc + 1],
                                            xr_b[:, sl, :], op0=OP.add, op1=OP.add)
                tev = C.inc(v, "dv")
                ps_o_free[sl] = tev
                xr_free[sl] = tev
                stg_free[sl] = C.dma(ge, f"oy{sl}",
                                     yt[dc * 128:(dc + 1) * 128, tb2 * 512:(tb2 + 1) * 512],
                                     stg[:, sl, :], [tev])
                if idx + 2 < 64:
                    load_wo_xr(idx + 2)
                drive_prep(1)
            barrier()
        esA.close()
        ty_all = [C.tok("oy0"), C.tok("oy1")]

        with ExitStack() as esB:
            sbB = lambda n, s, ty: esB.enter_context(nc.sbuf_tensor(n, s, ty))
            xh = sbB("xh", [128, DC, HALF], BF16)
            xl = sbB("xl", [128, DC, HALF], BF16)
            h_hi = sbB("h_hi", [128, 11, HALF], BF16)
            h_lo = sbB("h_lo", [128, 11, HALF], BF16)
            scr2 = sbB("scr2", [128, 2, HALF], F32)
            sgb = sbB("sgb", [128, HALF], F32)
            htmp = sbB("htmp", [128, HALF], F32)
            dwq = sbB("dwq", [128, 2816], I32)
            dzb = sbB("dzb", [128, 2, 86], F32)
            dsb = sbB("dsb", [128, 2, 86], F32)
            dwhiT = sbB("dwhiT", [128, 2, 11, 128], BF16)
            dwloT = sbB("dwloT", [128, 2, 11, 128], BF16)
            plod = sbB("plod", [128, 2, 512], F32)

            xh_tok = [None] * DC
            xl_tok = [None] * DC
            xhl_free = [None] * DC
            h_tok = [None] * 11
            h_free = [None] * 11
            dwq_tok = None
            dwq_free = None
            dzs_tok = [None, None]
            dzs_free = [None, None]
            dwT_tok = [None, None]
            dwT_free = [None, None]
            plod_free = [None, None]
            stgB_free = [None, None]
            scr2_free = [None, None]
            sgb_free = None
            htmp_free = None
            py_tok = {}
            sb_start = np.cumsum([0] + SBS)

            for half in range(2):
                hc0 = half * HALF
                sp_all = [C.tok(s) for s in ("sp0", "sp1", "spd0", "spd1")]
                with ExitStack() as es0:
                    ps_st2 = es0.enter_context(nc.psum_tensor(f"ps_st2_{half}", [1, 1024], F32))
                    ps_b2 = es0.enter_context(nc.psum_tensor(f"ps_b2_{half}", [128, 1024], F32))
                    for dc in range(DC):
                        sl = dc % 2
                        tx2 = C.dma(ge, f"x2{sl}", scr2[:, sl, :],
                                    yt[dc * 128:(dc + 1) * 128, hc0:hc0 + HALF],
                                    [ty_all, scr2_free[sl]])
                        C.waits(ae, [tx2, sgb_free])
                        a = ae.activation(sgb[:], scr2[:, sl, :], AF.Square)
                        tsq = C.inc(a, "ac")
                        C.waits(pe, [tx2, tsq, tconst])
                        pe.matmul(ps_st2[0:1, 0:HALF], onesc[:], scr2[:, sl, :],
                                  start=(dc == 0), stop=(dc == DC - 1))
                        m2 = pe.matmul(ps_st2[0:1, HALF:2 * HALF], onesc[:], sgb[:],
                                       start=(dc == 0), stop=(dc == DC - 1))
                        t = C.inc(m2, "pe")
                        sgb_free = t
                        scr2_free[sl] = [t, tsq]
                    tstat = C.tok("pe")
                    C.wait(ve, tstat)
                    v = ve.tensor_scalar(strow[0:1, 0:512], ps_st2[0:1, 0:512], 1.0 / D, None, op0=OP.mult)
                    C.inc(v, "dv")
                    v = ve.tensor_scalar(strow[0:1, 512:1024], ps_st2[0:1, 512:1024], 1.0 / D,
                                         None, op0=OP.mult)
                    tv = C.inc(v, "dv")
                    C.wait(ve, tv)
                    v = ve.tensor_tensor(strow[0:1, 1024:1536], strow[0:1, 0:512], strow[0:1, 0:512],
                                         op=OP.mult)
                    tv = C.inc(v, "dv")
                    C.wait(ve, tv)
                    v = ve.scalar_tensor_tensor(strow[0:1, 1536:2048], strow[0:1, 512:1024],
                                                EPS, strow[0:1, 1024:1536],
                                                op0=OP.add, op1=OP.subtract)
                    tv = C.inc(v, "dv")
                    C.wait(ae, tv)
                    a = ae.activation(strow[0:1, 1024:1536], strow[0:1, 1536:2048], AF.Sqrt)
                    ta = C.inc(a, "ac")
                    C.wait(ve, ta)
                    v = ve.reciprocal(strow[0:1, 512:1024], strow[0:1, 1024:1536])
                    trstd = C.inc(v, "dv")
                    C.waits(pe, [trstd, tconst])
                    pe.matmul(ps_b2[:, 0:512], onesr[:], strow[0:1, 0:512], start=True, stop=True)
                    mb = pe.matmul(ps_b2[:, 512:1024], onesr[:], strow[0:1, 512:1024],
                                   start=True, stop=True)
                    tbb = C.inc(mb, "pe")
                    C.wait(ve, tbb)
                    v = ve.tensor_copy(mrrep[:, 0:1024], ps_b2[:, 0:1024])
                    trep = C.inc(v, "dv")
                    for dc in range(DC):
                        sl = dc % 2
                        tx2 = C.dma(ge, f"x2{sl}", scr2[:, sl, :],
                                    yt[dc * 128:(dc + 1) * 128, hc0:hc0 + HALF],
                                    [scr2_free[sl]])
                        C.waits(ve, [tx2, trep, htmp_free])
                        v = ve.tensor_tensor(htmp[:], scr2[:, sl, :], mrrep[:, 0:512], op=OP.subtract)
                        tv = C.inc(v, "dv")
                        scr2_free[sl] = [tv]
                        C.waits(ve, [tv, sgb_free])
                        v = ve.tensor_tensor(sgb[:], htmp[:], mrrep[:, 512:1024], op=OP.mult)
                        tln = C.inc(v, "dv")
                        htmp_free = tln
                        C.waits(ae, [tln, xhl_free[dc]])
                        a = ae.activation(xh[:, dc, :], sgb[:], AF.Copy)
                        thi = C.inc(a, "ac")
                        xh_tok[dc] = thi
                        C.waits(ve, [thi])
                        v = ve.tensor_tensor(xl[:, dc, :], sgb[:], xh[:, dc, :], op=OP.subtract)
                        tlo = C.inc(v, "dv")
                        xl_tok[dc] = tlo
                        sgb_free = [thi, tlo]

                with ExitStack() as es12:
                    ps_g = es12.enter_context(nc.psum_tensor(f"ps_g_{half}", [128, 2, 512], F32))
                    ps_u = es12.enter_context(nc.psum_tensor(f"ps_u_{half}", [128, 2, 512], F32))
                    ps_d = es12.enter_context(nc.psum_tensor(f"ps_d_{half}", [128, 2, 512], F32))
                    ps_gu_free = [None, None]
                    ps_d_free = [None, None]
                    g_b1start = [prep_done[0]]

                    for sbi in range(NSB):
                        i0 = int(sb_start[sbi])
                        sbn = SBS[sbi]
                        for icn in range(sbn):
                            ic = i0 + icn
                            tg = tu = None
                            for wi in range(2):
                                dsl = wi
                                wsp_w = wsp_g if wi == 0 else wsp_u
                                s_idx = 2 * ic + wi
                                if half == 0 and s_idx >= g_b1start[0]:
                                    while prep_done[0] <= s_idx:
                                        drive_prep(1)
                                    sp_tok = [sp_done[s_idx]]
                                else:
                                    th_rl = C.dma(se, f"rl{dsl}", whiT[:, dsl, :, :], wsp_w[ic, 0],
                                                  [wT_free[dsl], sp_done[s_idx]])
                                    tl_rl = C.dma(se, f"rl{dsl}", wloT[:, dsl, :, :], wsp_w[ic, 1], [])
                                    wT_tok[dsl] = [[tl_rl]] * 8
                                    sp_tok = []
                                psd = ps_g if wi == 0 else ps_u
                                gsl = ic % 2
                                C.wait(pe, ps_gu_free[gsl])
                                for k in range(DC):
                                    C.waits(pe, [xh_tok[k], xl_tok[k], wT_tok[dsl][k // 4]])
                                    pe.matmul(psd[:, gsl, :], whiT[:, dsl, k, :], xh[:, k, :],
                                              start=(k == 0), stop=False)
                                    pe.matmul(psd[:, gsl, :], whiT[:, dsl, k, :], xl[:, k, :],
                                              start=False, stop=False)
                                    mm = pe.matmul(psd[:, gsl, :], wloT[:, dsl, k, :], xh[:, k, :],
                                                   start=False, stop=(k == DC - 1))
                                tmm = C.inc(mm, "pe")
                                wT_free[dsl] = [tmm] + wT_tok[dsl][-1] + sp_tok
                                if wi == 0:
                                    tg = tmm
                                else:
                                    tu = tmm
                            gsl = ic % 2
                            C.waits(ae, [tg, sgb_free])
                            a = ae.activation(sgb[:], ps_g[:, gsl, :], AF.Silu)
                            tsg = C.inc(a, "ac")
                            C.waits(ve, [tsg, tu, htmp_free])
                            v = ve.tensor_tensor(htmp[:], sgb[:], ps_u[:, gsl, :], op=OP.mult)
                            th = C.inc(v, "dv")
                            sgb_free = th
                            ps_gu_free[gsl] = [tsg, th]
                            C.waits(ae, [th, h_free[icn]])
                            a = ae.activation(h_hi[:, icn, :], htmp[:], AF.Copy)
                            thi = C.inc(a, "ac")
                            C.waits(ve, [thi, h_free[icn]])
                            v = ve.tensor_tensor(h_lo[:, icn, :], htmp[:], h_hi[:, icn, :], op=OP.subtract)
                            tlo = C.inc(v, "dv")
                            htmp_free = [thi, tlo]
                            h_tok[icn] = [thi, tlo]
                        t_sb_last_mm = None
                        for dc in range(DC):
                            dsl = dc % 2
                            if half == 1:
                                th_rl = C.dma(se, f"rld{dsl}", dwhiT[:, dsl, 0:sbn, :],
                                              wspd[sbi, dc, 0, :, 0:sbn * 128],
                                              [dwT_free[dsl], sp_all])
                                tl_rl = C.dma(se, f"rld{dsl}", dwloT[:, dsl, 0:sbn, :],
                                              wspd[sbi, dc, 1, :, 0:sbn * 128], [])
                                dwT_tok[dsl] = [[tl_rl]] * 3
                                dsp_tok = []
                            else:
                                dwq_tok = C.dma(se, "dwq", dwq[:, 0:sbn * 128],
                                                dqw[dc * 128:(dc + 1) * 128, i0 * 128:(i0 + sbn) * 128],
                                                [dwq_free])
                                zsl = dc % 2
                                C.dma(se, f"dzs{zsl}", dzb[:, zsl, :],
                                      dzt[dc * 128:(dc + 1) * 128, :], [dzs_free[zsl]])
                                dzs_tok[zsl] = C.dma(se, f"dzs{zsl}", dsb[:, zsl, :],
                                                     dst_[dc * 128:(dc + 1) * 128, :], [])
                            dbatches = []
                            old_free = dwT_free[dsl]
                            tdq_last = None
                            ttps = []

                            def devac(b):
                                psl = b % 2
                                b0 = b * 4
                                nb = min(4, sbn - b0)
                                C.waits(ae, [ttps[b], old_free])
                                a = ae.activation(dwhiT[:, dsl, b0:b0 + nb, :],
                                                  ps_tr[:, psl, 0:nb * 128], AF.Copy)
                                thi = C.inc(a, "ac")
                                C.waits(ve, [thi, ttps[b], old_free])
                                v = ve.tensor_tensor(dwloT[:, dsl, b0:b0 + nb, :],
                                                     ps_tr[:, psl, 0:nb * 128],
                                                     dwhiT[:, dsl, b0:b0 + nb, :], op=OP.subtract)
                                tlo = C.inc(v, "dv")
                                dbatches.append([thi, tlo])
                                ring_free[psl] = [thi, tlo]

                            if half == 0:
                                for j in range(sbn):
                                    wsl = j % 4
                                    C.waits(ve, [dwq_tok, dzs_tok[zsl], wtile_free[wsl]])
                                    v = ve.tensor_scalar(wtile[:, wsl, :], dwq[:, j * 128:(j + 1) * 128],
                                                         dzb[:, zsl, i0 + j:i0 + j + 1],
                                                         dsb[:, zsl, i0 + j:i0 + j + 1],
                                                         op0=OP.subtract, op1=OP.mult)
                                    tdq = C.inc(v, "dv")
                                    tdq_last = tdq
                                    psl = (j // 4) % 2
                                    if j % 4 == 0:
                                        C.wait(pe, ring_free[psl])
                                    C.waits(pe, [tdq, tconst])
                                    mm = pe.transpose(ps_tr[:, psl, (j % 4) * 128:(j % 4 + 1) * 128],
                                                      wtile[:, wsl, :], ident[:])
                                    ttp = C.inc(mm, "pe")
                                    wtile_free[wsl] = ttp
                                    if j % 4 == 3 or j == sbn - 1:
                                        ttps.append(ttp)
                                        if len(ttps) >= 2:
                                            devac(len(ttps) - 2)
                                devac(len(ttps) - 1)
                                dwq_free = tdq_last
                                dzs_free[zsl] = tdq_last
                                dwT_tok[dsl] = dbatches
                                th_sp = C.dma(ge, f"spd{dsl}", wspd[sbi, dc, 0, :, 0:sbn * 128],
                                              dwhiT[:, dsl, 0:sbn, :], [dbatches[-1][0]])
                                tl_sp = C.dma(ge, f"spd{dsl}", wspd[sbi, dc, 1, :, 0:sbn * 128],
                                              dwloT[:, dsl, 0:sbn, :], [dbatches[-1][1]])
                                dsp_tok = [tl_sp]
                            psl_d = dc % 2
                            C.wait(pe, ps_d_free[psl_d])
                            for icn in range(sbn):
                                C.waits(pe, [h_tok[icn], dwT_tok[dsl][icn // 4]])
                                pe.matmul(ps_d[:, psl_d, :], dwhiT[:, dsl, icn, :], h_hi[:, icn, :],
                                          start=(icn == 0), stop=False)
                                pe.matmul(ps_d[:, psl_d, :], dwhiT[:, dsl, icn, :], h_lo[:, icn, :],
                                          start=False, stop=False)
                                mm = pe.matmul(ps_d[:, psl_d, :], dwloT[:, dsl, icn, :], h_hi[:, icn, :],
                                               start=False, stop=(icn == sbn - 1))
                            tmm = C.inc(mm, "pe")
                            dwT_free[dsl] = [tmm] + dwT_tok[dsl][-1] + dsp_tok
                            t_sb_last_mm = tmm
                            ssl = dc % 2
                            if sbi == 0:
                                C.waits(ve, [tmm, stgB_free[ssl]])
                                v = ve.tensor_copy(stg[:, ssl, :], ps_d[:, psl_d, :])
                                tev = C.inc(v, "dv")
                                ps_d_free[psl_d] = tev
                                tst = C.dma(ge, f"op{ssl}", pscr[half, 0, dc * 128:(dc + 1) * 128, :],
                                            stg[:, ssl, :], [tev])
                                stgB_free[ssl] = tst
                                py_tok[(half, dc)] = tst
                            elif sbi < NSB - 1:
                                tpl = C.dma(ge, f"pl{ssl}", plod[:, ssl, :],
                                            pscr[half, 0, dc * 128:(dc + 1) * 128, :],
                                            [plod_free[ssl], py_tok[(half, dc)]])
                                C.waits(ve, [tmm, tpl, stgB_free[ssl]])
                                v = ve.tensor_tensor(stg[:, ssl, :], ps_d[:, psl_d, :],
                                                     plod[:, ssl, :], op=OP.add)
                                tev = C.inc(v, "dv")
                                ps_d_free[psl_d] = tev
                                plod_free[ssl] = tev
                                tst = C.dma(ge, f"op{ssl}", pscr[half, 0, dc * 128:(dc + 1) * 128, :],
                                            stg[:, ssl, :], [tev])
                                stgB_free[ssl] = tst
                                py_tok[(half, dc)] = tst
                            else:
                                tpl = C.dma(ge, f"pl{ssl}", plod[:, ssl, :],
                                            pscr[half, 0, dc * 128:(dc + 1) * 128, :],
                                            [plod_free[ssl], py_tok[(half, dc)]])
                                C.waits(ve, [tmm, tpl, stgB_free[ssl]])
                                v = ve.tensor_tensor(stg[:, ssl, :], ps_d[:, psl_d, :],
                                                     plod[:, ssl, :], op=OP.add)
                                tacc = C.inc(v, "dv")
                                ps_d_free[psl_d] = tacc
                                tx2 = C.dma(ge, f"x2{ssl}", scr2[:, ssl, :],
                                            yt[dc * 128:(dc + 1) * 128, hc0:hc0 + HALF],
                                            [scr2_free[ssl]])
                                C.waits(ve, [tacc, tx2])
                                v = ve.tensor_tensor(plod[:, ssl, :], stg[:, ssl, :],
                                                     scr2[:, ssl, :], op=OP.add)
                                tacc2 = C.inc(v, "dv")
                                stgB_free[ssl] = tacc2
                                scr2_free[ssl] = [tacc2]
                                tst = C.dma(ge, f"oy{ssl}", yt[dc * 128:(dc + 1) * 128, hc0:hc0 + HALF],
                                            plod[:, ssl, :], [tacc2])
                                plod_free[ssl] = tst
                        for icn in range(11):
                            if h_tok[icn] is not None:
                                h_free[icn] = [t_sb_last_mm] + h_tok[icn][1:]
                barrier()
                for dc in range(DC):
                    xhl_free[dc] = C.tok("pe")
            for sname in ("oy0", "oy1", "op0", "op1"):
                C.wait(ge, C.tok(sname))
    return nc


def _host_prep(inputs):
    x = np.asarray(inputs["L0_attn_x"], dtype=np.float32)
    wqk = np.asarray(inputs["L0_attn_Wqk_w"], dtype=np.float32)
    bqk = np.asarray(inputs["L0_attn_Wqk_b"], dtype=np.float32)
    wv = np.asarray(inputs["L0_attn_Wv_w"], dtype=np.float32)
    bv = np.asarray(inputs["L0_attn_Wv_b"], dtype=np.float32)
    wo = np.asarray(inputs["L0_attn_Wo_w"], dtype=np.float32)
    bo = np.asarray(inputs["L0_attn_Wo_b"], dtype=np.float32)

    perm = np.zeros(1024, dtype=np.int64)
    e = np.arange(64)
    for hp in range(4):
        h0, h1 = 2 * hp, 2 * hp + 1
        base = hp * 256
        perm[base + 0:base + 64] = h0 * 128 + 2 * e
        perm[base + 64:base + 128] = h1 * 128 + 2 * e
        perm[base + 128:base + 192] = h0 * 128 + 2 * e + 1
        perm[base + 192:base + 256] = h1 * 128 + 2 * e + 1

    common = dict(
        wqkt=np.ascontiguousarray(wqk[perm].T),
        bqk=np.ascontiguousarray(bqk[perm].reshape(8, 128).T),
        wvt=np.ascontiguousarray(wv.T),
        bvr=np.ascontiguousarray(np.broadcast_to(bv[None, :], (128, 512))),
        wot=np.ascontiguousarray(wo.T),
        bo=np.ascontiguousarray(bo.reshape(32, 128).T),
        gq=np.ascontiguousarray(inputs["L0_ffn_gate_qweight"], dtype=np.int32),
        gz=np.ascontiguousarray(inputs["L0_ffn_gate_zeros"], dtype=np.float32),
        gs=np.ascontiguousarray(inputs["L0_ffn_gate_scales"], dtype=np.float32),
        uq=np.ascontiguousarray(inputs["L0_ffn_up_qweight"], dtype=np.int32),
        uz=np.ascontiguousarray(inputs["L0_ffn_up_zeros"], dtype=np.float32),
        us=np.ascontiguousarray(inputs["L0_ffn_up_scales"], dtype=np.float32),
        dqw=np.ascontiguousarray(inputs["L0_ffn_down_qweight"], dtype=np.int32),
        dzt=np.ascontiguousarray(inputs["L0_ffn_down_zeros"], dtype=np.float32),
        dst=np.ascontiguousarray(inputs["L0_ffn_down_scales"], dtype=np.float32),
        identd=np.eye(128, dtype=np.float32),
        onescd=np.ones((128, 1), dtype=np.float32),
        onesrd=np.ones((1, 128), dtype=np.float32),
    )

    in_maps = []
    for c in range(8):
        b, hf = c // 2, c % 2
        xb = x[b].T
        own = xb[:, hf * 1024:(hf + 1) * 1024]
        other = xb[:, (1 - hf) * 1024:(2 - hf) * 1024]
        im = dict(common)
        im["xt"] = np.ascontiguousarray(np.concatenate([own, other], axis=1))
        in_maps.append(im)
    return in_maps


def kernel(**inputs):
    in_maps = _host_prep(inputs)
    nc = build_nc()
    res = run_bass_kernel_spmd(nc, in_maps, core_ids=list(range(8)))
    out = np.empty((4, 2048, 4096), dtype=np.float32)
    for c in range(8):
        b, hf = c // 2, c % 2
        out[b, hf * 1024:(hf + 1) * 1024, :] = res.results[c]["yt"].T
    return out


# version 07
# speedup vs baseline: 1.1565x; 1.0876x over previous
import numpy as np
from contextlib import ExitStack
import concourse.bass as bass
import concourse.mybir as mybir
from concourse.alu_op_type import AluOpType as OP
from concourse.bass_utils import run_bass_kernel_spmd

F32 = mybir.dt.float32
BF16 = mybir.dt.bfloat16
I32 = mybir.dt.int32
AF = mybir.ActivationFunctionType
AX = mybir.AxisListType

D = 4096
DC = 32
SEQ = 2048
OWN = 1024
IF = 11008
ICN = 86
H = 8
TBA = 256
NTB = SEQ // TBA
HALF = 512
SBS = [11, 11, 11, 11, 11, 11, 11, 9]
NSB = len(SBS)
INV_SQRT_HD = 1.0 / (512.0 ** 0.5)
EPS = 1e-5


class Ctl:

    def __init__(self, nc, es):
        self.nc = nc
        self.sem = {}
        self.count = {}
        self.hw = {}
        self.es = es

    def new(self, *names):
        for n in names:
            self.sem[n] = self.es.enter_context(self.nc.semaphore(n))
            self.count[n] = 0

    def inc(self, instr, name, n=1):
        self.count[name] += n
        instr.then_inc(self.sem[name], n)
        return (name, self.count[name])

    def tok(self, name):
        return (name, self.count[name])

    def wait(self, eng, tok):
        if tok is None:
            return
        if isinstance(tok, (list, tuple)) and tok and isinstance(tok[0], (list, tuple, type(None))):
            for t in tok:
                self.wait(eng, t)
            return
        name, val = tok
        if val <= 0:
            return
        key = (id(eng), name)
        if self.hw.get(key, 0) >= val:
            return
        eng.wait_ge(self.sem[name], val)
        self.hw[key] = val

    def waits(self, eng, toks):
        for t in toks:
            self.wait(eng, t)

    def dma(self, eng, sem, out, in_, frees=()):
        for f in frees:
            self.wait(eng, f)
        d = eng.dma_start(out=out, in_=in_)
        return self.inc(d, sem, 16)


def build_nc():
    nc = bass.Bass("TRN2", target_bir_lowering=False)
    dram = lambda n, s, ty, k: nc.dram_tensor(n, s, ty, kind=k).ap()
    xt = dram("xt", [D, SEQ], F32, "ExternalInput")
    wqkt = dram("wqkt", [D, 1024], F32, "ExternalInput")
    bqk = dram("bqk", [128, 8], F32, "ExternalInput")
    wvt = dram("wvt", [D, 512], F32, "ExternalInput")
    bvr = dram("bvr", [128, 512], F32, "ExternalInput")
    wot = dram("wot", [512, D], F32, "ExternalInput")
    bo = dram("bo", [128, 32], F32, "ExternalInput")
    gq = dram("gq", [IF, D], I32, "ExternalInput")
    gz = dram("gz", [IF, 32], F32, "ExternalInput")
    gs = dram("gs", [IF, 32], F32, "ExternalInput")
    uq = dram("uq", [IF, D], I32, "ExternalInput")
    uz = dram("uz", [IF, 32], F32, "ExternalInput")
    us = dram("us", [IF, 32], F32, "ExternalInput")
    dqw = dram("dqw", [D, IF], I32, "ExternalInput")
    dzt = dram("dzt", [D, 86], F32, "ExternalInput")
    dst_ = dram("dst", [D, 86], F32, "ExternalInput")
    identd = dram("identd", [128, 128], F32, "ExternalInput")
    onescd = dram("onescd", [128, 1], F32, "ExternalInput")
    onesrd = dram("onesrd", [1, 128], F32, "ExternalInput")
    yt = dram("yt", [D, OWN], F32, "ExternalOutput")
    pscr = dram("pscr", [2, NSB - 1, D, HALF], F32, "Internal")
    wsp_g = dram("wsp_g", [ICN, 2, 128, DC * 128], BF16, "Internal")
    wsp_u = dram("wsp_u", [ICN, 2, 128, DC * 128], BF16, "Internal")
    wspd = dram("wspd", [NSB, DC, 2, 128, 11 * 128], BF16, "Internal")

    with ExitStack() as es:
        C = Ctl(nc, es)
        C.new("pe", "dv", "ac", "bar", "d_const",
              "wqk0", "wqk1", "wv0", "wv1", "wo0", "wo1", "xr0", "xr1",
              "x20", "x21", "wqA0", "wqA1", "zs0", "zs1", "dwq",
              "dzs0", "dzs1", "pl0", "pl1", "oy0", "oy1", "op0", "op1",
              "sp0", "sp1", "spd0", "spd1", "rl0", "rl1", "rld0", "rld1")
        C.new(*[f"xb{i}" for i in range(DC)])

        pe, ve, ae, ge, se = nc.tensor, nc.vector, nc.scalar, nc.gpsimd, nc.sync
        ENGS = [pe, ve, ae, ge, se]
        sbuf = lambda n, s, ty: es.enter_context(nc.sbuf_tensor(n, s, ty))

        ident = sbuf("ident", [128, 128], F32)
        onesc = sbuf("onesc", [128, 1], F32)
        onesr = sbuf("onesr", [1, 128], F32)
        bqk_s = sbuf("bqk_s", [128, 8], F32)
        bvr_s = sbuf("bvr_s", [128, 512], F32)
        bo_s = sbuf("bo_s", [128, 32], F32)
        strow = sbuf("strow", [1, 2048], F32)
        mrrep = sbuf("mrrep", [128, 1024], F32)
        nm_s = sbuf("nm_s", [128, 1], F32)
        rs_s = sbuf("rs_s", [128, 1], F32)
        ri_s = sbuf("ri_s", [128, 1], F32)
        stg = sbuf("stg", [128, 2, 512], F32)

        def barrier():
            names = [n for n in C.count if C.count[n] > 0]
            for e in ENGS:
                for n in names:
                    C.wait(e, (n, C.count[n]))

        for dst_t, src in ((ident, identd), (onesc, onescd), (onesr, onesrd),
                           (bqk_s, bqk), (bvr_s, bvr), (bo_s, bo)):
            d = se.dma_start(out=dst_t[:], in_=src)
            C.inc(d, "d_const", 16)
        tconst = C.tok("d_const")

        esA = ExitStack()
        sbA = lambda n, s, ty: esA.enter_context(nc.sbuf_tensor(n, s, ty))
        q_sb = sbA("q_sb", [128, 4, 1024], F32)
        k_sb = sbA("k_sb", [128, 4, 2048], F32)
        v_sb = sbA("v_sb", [128, 16, 512], F32)
        pvt = sbA("pvt", [128, 4, 1024], F32)

        with ExitStack() as es1:
            sb1 = lambda n, s, ty: es1.enter_context(nc.sbuf_tensor(n, s, ty))
            xblk = sb1("xblk", [128, DC, TBA], F32)
            xln = sb1("xln", [128, DC, TBA], F32)
            sqb = sb1("sqb", [128, 2, TBA], F32)
            tmpb = sb1("tmpb", [128, 2, TBA], F32)
            wqk_b = sb1("wqk_b", [128, 2, 512], F32)
            wv_b = sb1("wv_b", [128, 2, 512], F32)
            ps_stat = es1.enter_context(nc.psum_tensor("ps_stat", [1, 768], F32))
            ps_qk = es1.enter_context(nc.psum_tensor("ps_qk", [128, 4, 512], F32))
            ps_v = es1.enter_context(nc.psum_tensor("ps_v", [128, 2, 512], F32))

            xblk_free = [None] * DC
            xln_free = [None] * DC
            xln_tok = [None] * DC
            sq_free = [None, None]
            tmp_free = [None, None]
            wqk_tok = [None, None]
            wqk_free = [None, None]
            wv_tok = [None, None]
            wv_free = [None, None]
            ps_stat_free = None
            ps_qk_free = None
            ps_v_free = None

            wqk_seq = [(t_, p_, k_) for t_ in range(NTB) for p_ in range(2) for k_ in range(DC)]
            wqk_ptr = [0]

            def load_wqk_next():
                if wqk_ptr[0] >= len(wqk_seq):
                    return
                t_, p_, k_ = wqk_seq[wqk_ptr[0]]
                sl = wqk_ptr[0] % 2
                wqk_ptr[0] += 1
                wqk_tok[sl] = C.dma(se, f"wqk{sl}", wqk_b[:, sl, :],
                                    wqkt[k_ * 128:(k_ + 1) * 128, p_ * 512:(p_ + 1) * 512],
                                    [wqk_free[sl]])

            wv_seq = [(t_, k_) for t_ in range(NTB) for k_ in range(DC)]
            wv_ptr = [0]

            def load_wv_next():
                if wv_ptr[0] >= len(wv_seq):
                    return
                t_, k_ = wv_seq[wv_ptr[0]]
                sl = wv_ptr[0] % 2
                wv_ptr[0] += 1
                wv_tok[sl] = C.dma(se, f"wv{sl}", wv_b[:, sl, :],
                                   wvt[k_ * 128:(k_ + 1) * 128, :], [wv_free[sl]])

            load_wqk_next()
            load_wqk_next()
            load_wv_next()
            load_wv_next()
            wqk_use = [0]
            wv_use = [0]

            for tb in range(NTB):
                xblk_tok = [None] * DC
                for dc in range(DC):
                    xblk_tok[dc] = C.dma(ge, f"xb{dc}", xblk[:, dc, :],
                                         xt[dc * 128:(dc + 1) * 128, tb * TBA:(tb + 1) * TBA],
                                         [xblk_free[dc]])
                C.wait(pe, ps_stat_free)
                for dc in range(DC):
                    sl = dc % 2
                    C.waits(ae, [xblk_tok[dc], sq_free[sl]])
                    a = ae.activation(sqb[:, sl, :], xblk[:, dc, :], AF.Square)
                    tsq = C.inc(a, "ac")
                    C.waits(pe, [xblk_tok[dc], tsq, tconst])
                    pe.matmul(ps_stat[0:1, 0:TBA], onesc[:], xblk[:, dc, :],
                              start=(dc == 0), stop=(dc == DC - 1))
                    m2 = pe.matmul(ps_stat[0:1, 512:512 + TBA], onesc[:], sqb[:, sl, :],
                                   start=(dc == 0), stop=(dc == DC - 1))
                    t = C.inc(m2, "pe")
                    sq_free[sl] = t
                    xblk_free[dc] = [t, tsq]
                tstat = C.tok("pe")
                C.wait(ve, tstat)
                v = ve.tensor_scalar(strow[0:1, 0:TBA], ps_stat[0:1, 0:TBA], 1.0 / D, None, op0=OP.mult)
                C.inc(v, "dv")
                v = ve.tensor_scalar(strow[0:1, 256:256 + TBA], ps_stat[0:1, 512:512 + TBA], 1.0 / D,
                                     None, op0=OP.mult)
                tv = C.inc(v, "dv")
                ps_stat_free = tv
                C.wait(ve, tv)
                v = ve.tensor_tensor(strow[0:1, 512:512 + TBA], strow[0:1, 0:TBA], strow[0:1, 0:TBA],
                                     op=OP.mult)
                tv = C.inc(v, "dv")
                C.wait(ve, tv)
                v = ve.scalar_tensor_tensor(strow[0:1, 768:768 + TBA], strow[0:1, 256:256 + TBA],
                                            EPS, strow[0:1, 512:512 + TBA],
                                            op0=OP.add, op1=OP.subtract)
                tv = C.inc(v, "dv")
                C.wait(ae, tv)
                a = ae.activation(strow[0:1, 1024:1024 + TBA], strow[0:1, 768:768 + TBA], AF.Sqrt)
                ta = C.inc(a, "ac")
                C.wait(ve, ta)
                v = ve.reciprocal(strow[0:1, 1280:1280 + TBA], strow[0:1, 1024:1024 + TBA])
                trstd = C.inc(v, "dv")
                C.waits(pe, [trstd, tconst, ps_v_free])
                pe.matmul(ps_v[:, 0, 0:TBA], onesr[:], strow[0:1, 0:TBA], start=True, stop=True)
                mb = pe.matmul(ps_v[:, 0, 256:256 + TBA], onesr[:], strow[0:1, 1280:1280 + TBA],
                               start=True, stop=True)
                tbb = C.inc(mb, "pe")
                C.wait(ve, tbb)
                v = ve.tensor_copy(mrrep[:, 0:512], ps_v[:, 0, 0:512])
                trep = C.inc(v, "dv")
                tlast_k = None
                for k in range(DC):
                    sl = k % 2
                    C.waits(ve, [xblk_tok[k], trep, tmp_free[sl]])
                    v = ve.tensor_tensor(tmpb[:, sl, :], xblk[:, k, :], mrrep[:, 0:TBA], op=OP.subtract)
                    tv = C.inc(v, "dv")
                    xblk_free[k] = xblk_free[k] + [tv]
                    C.waits(ve, [tv, xln_free[k]])
                    v = ve.tensor_tensor(xln[:, k, :], tmpb[:, sl, :], mrrep[:, 256:256 + TBA], op=OP.mult)
                    txln = C.inc(v, "dv")
                    tmp_free[sl] = txln
                    xln_tok[k] = txln
                    wsl = wqk_use[0] % 2
                    wqk_use[0] += 1
                    vsl = wv_use[0] % 2
                    wv_use[0] += 1
                    C.waits(pe, [txln, wqk_tok[wsl], wv_tok[vsl], ps_qk_free, ps_v_free])
                    for m in range(4):
                        pe.matmul(ps_qk[:, m, 0:TBA], wqk_b[:, wsl, m * 128:(m + 1) * 128],
                                  xln[:, k, :], start=(k == 0), stop=(k == DC - 1))
                    for ts in range(2):
                        mm = pe.matmul(ps_v[:, ts, :], xln[:, k, ts * 128:(ts + 1) * 128],
                                       wv_b[:, vsl, :], start=(k == 0), stop=(k == DC - 1))
                    tlast_k = C.inc(mm, "pe")
                    wqk_free[wsl] = tlast_k
                    wv_free[vsl] = tlast_k
                    load_wqk_next()
                    load_wv_next()
                C.waits(ve, [tlast_k, tconst])
                for ts in range(2):
                    v = ve.tensor_tensor(v_sb[:, tb * 2 + ts, :], ps_v[:, ts, :], bvr_s[:], op=OP.add)
                    C.inc(v, "dv")
                ps_v_free = C.tok("dv")
                for m in range(4):
                    hp = m // 2
                    if m % 2 == 0:
                        if tb < 4:
                            v = ve.tensor_scalar(q_sb[:, hp, tb * TBA:(tb + 1) * TBA], ps_qk[:, m, 0:TBA],
                                                 bqk_s[:, m:m + 1], None, op0=OP.add)
                            C.inc(v, "dv")
                    else:
                        v = ve.tensor_scalar(k_sb[:, hp, tb * TBA:(tb + 1) * TBA], ps_qk[:, m, 0:TBA],
                                             bqk_s[:, m:m + 1], None, op0=OP.add)
                        C.inc(v, "dv")
                ps_qk_free = C.tok("dv")
                tlast_k = None
                for k in range(DC):
                    wsl = wqk_use[0] % 2
                    wqk_use[0] += 1
                    C.waits(pe, [xln_tok[k], wqk_tok[wsl], ps_qk_free])
                    for m in range(4):
                        mm = pe.matmul(ps_qk[:, m, 0:TBA], wqk_b[:, wsl, m * 128:(m + 1) * 128],
                                       xln[:, k, :], start=(k == 0), stop=(k == DC - 1))
                    tlast_k = C.inc(mm, "pe")
                    wqk_free[wsl] = tlast_k
                    xln_free[k] = tlast_k
                    load_wqk_next()
                C.waits(ve, [tlast_k, tconst])
                for m in range(4):
                    mg = m + 4
                    hp = mg // 2
                    if mg % 2 == 0:
                        if tb < 4:
                            v = ve.tensor_scalar(q_sb[:, hp, tb * TBA:(tb + 1) * TBA], ps_qk[:, m, 0:TBA],
                                                 bqk_s[:, mg:mg + 1], None, op0=OP.add)
                            C.inc(v, "dv")
                    else:
                        v = ve.tensor_scalar(k_sb[:, hp, tb * TBA:(tb + 1) * TBA], ps_qk[:, m, 0:TBA],
                                             bqk_s[:, mg:mg + 1], None, op0=OP.add)
                        C.inc(v, "dv")
                ps_qk_free = C.tok("dv")
            barrier()

        sbufR = lambda n, s, ty: es.enter_context(nc.sbuf_tensor(n, s, ty, side="right"))
        wq_b = sbufR("wq_b", [128, 2, 2048], I32)
        zb = sbufR("zb", [128, 2, 32], F32)
        szb = sbufR("szb", [128, 2, 32], F32)
        wtile = sbufR("wtile", [128, 4, 128], F32)
        whiT = sbufR("whiT", [128, 2, DC, 128], BF16)
        wloT = sbufR("wloT", [128, 2, DC, 128], BF16)
        ps_tr = es.enter_context(nc.psum_tensor("ps_tr", [128, 2, 512], F32))
        ring_free = [None, None]
        wq_tok = [None, None]
        wq_free = [None, None]
        zs_tok = [None, None]
        zs_free = [None, None]
        wtile_free = [None] * 4
        wT_tok = [None, None]
        wT_free = [None, None]
        sp_done = {}
        prep_done = [0]

        def strip_load(ic, wi):
            qsrc = gq if wi == 0 else uq
            zsrc = gz if wi == 0 else uz
            ssrc = gs if wi == 0 else us
            for hh in range(2):
                wq_tok[hh] = C.dma(se, f"wqA{hh}", wq_b[:, hh, :],
                                   qsrc[ic * 128:(ic + 1) * 128, hh * 2048:(hh + 1) * 2048],
                                   [wq_free[hh]])
            sl = wi
            C.dma(se, f"zs{sl}", zb[:, sl, :], zsrc[ic * 128:(ic + 1) * 128, :], [zs_free[sl]])
            zs_tok[sl] = C.dma(se, f"zs{sl}", szb[:, sl, :], ssrc[ic * 128:(ic + 1) * 128, :], [])

        def prep_strip_gen():
            strip_load(0, 0)
            for s in range(2 * ICN):
                ic, wi = s // 2, s % 2
                dsl = wi
                zsl = wi
                batches = []
                old_free = wT_free[dsl]
                ttps = []

                def evac(b):
                    psl = b % 2
                    j4 = b * 4
                    C.waits(ae, [ttps[b], old_free])
                    a = ae.activation(whiT[:, dsl, j4:j4 + 4, :], ps_tr[:, psl, :], AF.Copy)
                    thi = C.inc(a, "ac")
                    C.waits(ve, [thi, ttps[b], old_free])
                    v = ve.tensor_tensor(wloT[:, dsl, j4:j4 + 4, :], ps_tr[:, psl, :],
                                         whiT[:, dsl, j4:j4 + 4, :], op=OP.subtract)
                    tlo = C.inc(v, "dv")
                    batches.append([thi, tlo])
                    ring_free[psl] = [thi, tlo]

                for j in range(DC):
                    wsl = j % 4
                    hh = j // 16
                    C.waits(ve, [wq_tok[hh], zs_tok[zsl], wtile_free[wsl]])
                    v = ve.tensor_scalar(wtile[:, wsl, :], wq_b[:, hh, (j % 16) * 128:(j % 16 + 1) * 128],
                                         zb[:, zsl, j:j + 1], szb[:, zsl, j:j + 1],
                                         op0=OP.subtract, op1=OP.mult)
                    tdq = C.inc(v, "dv")
                    if j == 15:
                        wq_free[0] = tdq
                    if j == DC - 1:
                        wq_free[1] = tdq
                        zs_free[zsl] = tdq
                    psl = (j // 4) % 2
                    if j % 4 == 0:
                        C.wait(pe, ring_free[psl])
                    C.waits(pe, [tdq, tconst])
                    mm = pe.transpose(ps_tr[:, psl, (j % 4) * 128:(j % 4 + 1) * 128],
                                      wtile[:, wsl, :], ident[:])
                    ttp = C.inc(mm, "pe")
                    wtile_free[wsl] = ttp
                    if j % 4 == 3:
                        ttps.append(ttp)
                        if len(ttps) >= 2:
                            evac(len(ttps) - 2)
                evac(7)
                wT_tok[dsl] = batches
                nic, nwi = (ic, 1) if wi == 0 else (ic + 1, 0)
                if nic < ICN:
                    strip_load(nic, nwi)
                wsp_w = wsp_g if wi == 0 else wsp_u
                C.dma(ge, f"sp{dsl}", wsp_w[ic, 0], whiT[:, dsl, :, :], [batches[-1][0]])
                tl_sp = C.dma(ge, f"sp{dsl}", wsp_w[ic, 1], wloT[:, dsl, :, :], [batches[-1][1]])
                wT_free[dsl] = [tl_sp] + batches[-1]
                sp_done[s] = tl_sp
                prep_done[0] = s + 1
                yield

        prep_gen = prep_strip_gen()

        def drive_prep(n=1):
            for _ in range(n):
                if next(prep_gen, "END") == "END":
                    return

        with ExitStack() as es2:
            sb2 = lambda n, s, ty: es2.enter_context(nc.sbuf_tensor(n, s, ty))
            P_sb = sb2("P_sb", [128, 2048], F32)
            P2_sb = sb2("P2_sb", [128, 2048], F32)
            PT_sb = sb2("PT_sb", [128, 16, 128], F32)
            ps_s = es2.enter_context(nc.psum_tensor("ps_s", [128, 2048], F32))
            ps_pv = es2.enter_context(nc.psum_tensor("ps_pv", [128, 2, 512], F32))

            ps_s_free = None
            ps_pv_free = [None, None]
            P_free = None
            P2_free = None
            PT_free = None
            nm_free = None
            rs_free = None
            ri_free = None
            it = 0
            for h in range(H):
                bp = (h % 2) * 64
                hp = h // 2
                for qb in range(8):
                    C.wait(pe, ps_s_free)
                    for kt in range(4):
                        mm = pe.matmul(ps_s[:, kt * 512:(kt + 1) * 512],
                                       q_sb[bp:bp + 64, hp, qb * 128:(qb + 1) * 128],
                                       k_sb[bp:bp + 64, hp, kt * 512:(kt + 1) * 512],
                                       start=True, stop=True)
                    tsc = C.inc(mm, "pe")
                    C.waits(ve, [tsc, nm_free])
                    v = ve.reduce_max(nm_s[:], ps_s[:], axis=AX.X, negate=True)
                    tnm = C.inc(v, "dv")
                    C.waits(ae, [tnm, P_free, rs_free])
                    a = ae.activation(P_sb[:], ps_s[:], AF.Exp, bias=nm_s[:], scale=1.0, accum_out=rs_s[:])
                    texp = C.inc(a, "ac")
                    ps_s_free = [texp, tnm]
                    C.waits(ve, [texp, ri_free])
                    v = ve.tensor_scalar(nm_s[:], rs_s[:], 512.0 ** 0.5, None, op0=OP.mult)
                    trs2 = C.inc(v, "dv")
                    C.wait(ve, trs2)
                    v = ve.reciprocal(ri_s[:], nm_s[:])
                    tri = C.inc(v, "dv")
                    rs_free = tri
                    nm_free = tri
                    if it % 2 == 0:
                        drive_prep(1)
                    C.waits(ae, [tri, P2_free])
                    a = ae.activation(P2_sb[:], P_sb[:], AF.Copy, scale=ri_s[:])
                    tp2 = C.inc(a, "ac")
                    P_free = tp2
                    ri_free = tp2
                    C.waits(pe, [tp2, tconst])
                    tevac = None
                    tlast_tp = None
                    for j in range(16):
                        sl = (j // 4) % 2
                        if j % 4 == 0:
                            C.wait(pe, ring_free[sl])
                        mm = pe.transpose(ps_tr[:, sl, (j % 4) * 128:(j % 4 + 1) * 128],
                                          P2_sb[:, j * 128:(j + 1) * 128], ident[:])
                        if j % 4 == 3:
                            tlast_tp = C.inc(mm, "pe")
                            C.waits(ae, [tlast_tp, PT_free])
                            a = ae.activation(PT_sb[:, j - 3:j + 1, :], ps_tr[:, sl, :], AF.Copy)
                            tevac = C.inc(a, "ac")
                            ring_free[sl] = tevac
                    P2_free = tlast_tp
                    sl = it % 2
                    C.waits(pe, [tevac, ps_pv_free[sl]])
                    for tc in range(16):
                        mm = pe.matmul(ps_pv[bp:bp + 64, sl, 0:128], v_sb[:, tc, h * 64:(h + 1) * 64],
                                       PT_sb[:, tc, :], start=(tc == 0), stop=(tc == 15),
                                       tile_position=(0, bp))
                    tpv = C.inc(mm, "pe")
                    PT_free = tpv
                    C.wait(ve, tpv)
                    v = ve.tensor_copy(pvt[bp:bp + 64, hp, qb * 128:(qb + 1) * 128],
                                       ps_pv[bp:bp + 64, sl, 0:128])
                    ps_pv_free[sl] = C.inc(v, "dv")
                    it += 1
            barrier()

        with ExitStack() as es3:
            sb3 = lambda n, s, ty: es3.enter_context(nc.sbuf_tensor(n, s, ty))
            wo_b = sb3("wo_b", [128, 2, 4, 128], F32)
            xr_b = sb3("xr_b", [128, 2, 512], F32)
            ps_o = es3.enter_context(nc.psum_tensor("ps_o", [128, 2, 512], F32))

            wo_tok = [None, None]
            wo_free = [None, None]
            xr_tok = [None, None]
            xr_free = [None, None]
            ps_o_free = [None, None]
            stg_free = [None, None]
            tpvt_all = C.tok("dv")
            wot_r = wot.rearrange("(ec p) d -> p ec d", p=128)

            def load_wo_xr(idx):
                tb2, dc = idx // 32, idx % 32
                sl = idx % 2
                wo_tok[sl] = C.dma(se, f"wo{sl}", wo_b[:, sl, :, :],
                                   wot_r[:, :, dc * 128:(dc + 1) * 128], [wo_free[sl]])
                xr_tok[sl] = C.dma(ge, f"xr{sl}", xr_b[:, sl, :],
                                   xt[dc * 128:(dc + 1) * 128, tb2 * 512:(tb2 + 1) * 512],
                                   [xr_free[sl]])

            load_wo_xr(0)
            load_wo_xr(1)
            for idx in range(64):
                tb2, dc = idx // 32, idx % 32
                sl = idx % 2
                C.waits(pe, [wo_tok[sl], tpvt_all, ps_o_free[sl]])
                for ec in range(4):
                    mm = pe.matmul(ps_o[:, sl, :], wo_b[:, sl, ec, :],
                                   pvt[:, ec, tb2 * 512:(tb2 + 1) * 512],
                                   start=(ec == 0), stop=(ec == 3))
                tmm = C.inc(mm, "pe")
                wo_free[sl] = tmm
                C.waits(ve, [tmm, xr_tok[sl], tconst, stg_free[sl]])
                v = ve.scalar_tensor_tensor(stg[:, sl, :], ps_o[:, sl, :], bo_s[:, dc:dc + 1],
                                            xr_b[:, sl, :], op0=OP.add, op1=OP.add)
                tev = C.inc(v, "dv")
                ps_o_free[sl] = tev
                xr_free[sl] = tev
                stg_free[sl] = C.dma(ge, f"oy{sl}",
                                     yt[dc * 128:(dc + 1) * 128, tb2 * 512:(tb2 + 1) * 512],
                                     stg[:, sl, :], [tev])
                if idx + 2 < 64:
                    load_wo_xr(idx + 2)
                drive_prep(1)
            barrier()
        esA.close()
        ty_all = [C.tok("oy0"), C.tok("oy1")]

        with ExitStack() as esB:
            sbB = lambda n, s, ty: esB.enter_context(nc.sbuf_tensor(n, s, ty))
            xh = sbB("xh", [128, DC, HALF], BF16)
            xl = sbB("xl", [128, DC, HALF], BF16)
            h_hi = sbB("h_hi", [128, 11, HALF], BF16)
            h_lo = sbB("h_lo", [128, 11, HALF], BF16)
            scr2 = sbB("scr2", [128, 2, HALF], F32)
            sgb = sbB("sgb", [128, HALF], F32)
            htmp = sbB("htmp", [128, HALF], F32)
            dwq = sbB("dwq", [128, 2816], I32)
            dzb = sbB("dzb", [128, 2, 86], F32)
            dsb = sbB("dsb", [128, 2, 86], F32)
            dwhiT = sbB("dwhiT", [128, 2, 11, 128], BF16)
            dwloT = sbB("dwloT", [128, 2, 11, 128], BF16)
            plod = sbB("plod", [128, 2, 512], F32)

            xh_tok = [None] * DC
            xl_tok = [None] * DC
            xhl_free = [None] * DC
            h_tok = [None] * 11
            h_free = [None] * 11
            dwq_tok = None
            dwq_free = None
            dzs_tok = [None, None]
            dzs_free = [None, None]
            dwT_tok = [None, None]
            dwT_free = [None, None]
            plod_free = [None, None]
            stgB_free = [None, None]
            scr2_free = [None, None]
            sgb_free = None
            htmp_free = None
            py_tok = {}
            sb_start = np.cumsum([0] + SBS)

            for half in range(2):
                hc0 = half * HALF
                sp_all = [C.tok(s) for s in ("sp0", "sp1", "spd0", "spd1")]
                with ExitStack() as es0:
                    ps_st2 = es0.enter_context(nc.psum_tensor(f"ps_st2_{half}", [1, 1024], F32))
                    ps_b2 = es0.enter_context(nc.psum_tensor(f"ps_b2_{half}", [128, 1024], F32))
                    for dc in range(DC):
                        sl = dc % 2
                        tx2 = C.dma(ge, f"x2{sl}", scr2[:, sl, :],
                                    yt[dc * 128:(dc + 1) * 128, hc0:hc0 + HALF],
                                    [ty_all, scr2_free[sl]])
                        C.waits(ae, [tx2, sgb_free])
                        a = ae.activation(sgb[:], scr2[:, sl, :], AF.Square)
                        tsq = C.inc(a, "ac")
                        C.waits(pe, [tx2, tsq, tconst])
                        pe.matmul(ps_st2[0:1, 0:HALF], onesc[:], scr2[:, sl, :],
                                  start=(dc == 0), stop=(dc == DC - 1))
                        m2 = pe.matmul(ps_st2[0:1, HALF:2 * HALF], onesc[:], sgb[:],
                                       start=(dc == 0), stop=(dc == DC - 1))
                        t = C.inc(m2, "pe")
                        sgb_free = t
                        scr2_free[sl] = [t, tsq]
                    tstat = C.tok("pe")
                    C.wait(ve, tstat)
                    v = ve.tensor_scalar(strow[0:1, 0:512], ps_st2[0:1, 0:512], 1.0 / D, None, op0=OP.mult)
                    C.inc(v, "dv")
                    v = ve.tensor_scalar(strow[0:1, 512:1024], ps_st2[0:1, 512:1024], 1.0 / D,
                                         None, op0=OP.mult)
                    tv = C.inc(v, "dv")
                    C.wait(ve, tv)
                    v = ve.tensor_tensor(strow[0:1, 1024:1536], strow[0:1, 0:512], strow[0:1, 0:512],
                                         op=OP.mult)
                    tv = C.inc(v, "dv")
                    C.wait(ve, tv)
                    v = ve.scalar_tensor_tensor(strow[0:1, 1536:2048], strow[0:1, 512:1024],
                                                EPS, strow[0:1, 1024:1536],
                                                op0=OP.add, op1=OP.subtract)
                    tv = C.inc(v, "dv")
                    C.wait(ae, tv)
                    a = ae.activation(strow[0:1, 1024:1536], strow[0:1, 1536:2048], AF.Sqrt)
                    ta = C.inc(a, "ac")
                    C.wait(ve, ta)
                    v = ve.reciprocal(strow[0:1, 512:1024], strow[0:1, 1024:1536])
                    trstd = C.inc(v, "dv")
                    C.waits(pe, [trstd, tconst])
                    pe.matmul(ps_b2[:, 0:512], onesr[:], strow[0:1, 0:512], start=True, stop=True)
                    mb = pe.matmul(ps_b2[:, 512:1024], onesr[:], strow[0:1, 512:1024],
                                   start=True, stop=True)
                    tbb = C.inc(mb, "pe")
                    C.wait(ve, tbb)
                    v = ve.tensor_copy(mrrep[:, 0:1024], ps_b2[:, 0:1024])
                    trep = C.inc(v, "dv")
                    for dc in range(DC):
                        sl = dc % 2
                        tx2 = C.dma(ge, f"x2{sl}", scr2[:, sl, :],
                                    yt[dc * 128:(dc + 1) * 128, hc0:hc0 + HALF],
                                    [scr2_free[sl]])
                        C.waits(ve, [tx2, trep, htmp_free])
                        v = ve.tensor_tensor(htmp[:], scr2[:, sl, :], mrrep[:, 0:512], op=OP.subtract)
                        tv = C.inc(v, "dv")
                        scr2_free[sl] = [tv]
                        C.waits(ve, [tv, sgb_free])
                        v = ve.tensor_tensor(sgb[:], htmp[:], mrrep[:, 512:1024], op=OP.mult)
                        tln = C.inc(v, "dv")
                        htmp_free = tln
                        C.waits(ae, [tln, xhl_free[dc]])
                        a = ae.activation(xh[:, dc, :], sgb[:], AF.Copy)
                        thi = C.inc(a, "ac")
                        xh_tok[dc] = thi
                        C.waits(ve, [thi])
                        v = ve.tensor_tensor(xl[:, dc, :], sgb[:], xh[:, dc, :], op=OP.subtract)
                        tlo = C.inc(v, "dv")
                        xl_tok[dc] = tlo
                        sgb_free = [thi, tlo]

                with ExitStack() as es12:
                    ps_g = es12.enter_context(nc.psum_tensor(f"ps_g_{half}", [128, 2, 512], F32))
                    ps_u = es12.enter_context(nc.psum_tensor(f"ps_u_{half}", [128, 2, 512], F32))
                    ps_d = es12.enter_context(nc.psum_tensor(f"ps_d_{half}", [128, 2, 512], F32))
                    ps_gu_free = [None, None]
                    ps_d_free = [None, None]
                    g_b1start = [prep_done[0]]

                    for sbi in range(NSB):
                        i0 = int(sb_start[sbi])
                        sbn = SBS[sbi]
                        for icn in range(sbn):
                            ic = i0 + icn
                            tg = tu = None
                            for wi in range(2):
                                dsl = wi
                                wsp_w = wsp_g if wi == 0 else wsp_u
                                s_idx = 2 * ic + wi
                                if half == 0 and s_idx >= g_b1start[0]:
                                    while prep_done[0] <= s_idx:
                                        drive_prep(1)
                                    sp_tok = [sp_done[s_idx]]
                                else:
                                    th_rl = C.dma(se, f"rl{dsl}", whiT[:, dsl, :, :], wsp_w[ic, 0],
                                                  [wT_free[dsl], sp_done[s_idx]])
                                    tl_rl = C.dma(se, f"rl{dsl}", wloT[:, dsl, :, :], wsp_w[ic, 1], [])
                                    wT_tok[dsl] = [[tl_rl]] * 8
                                    sp_tok = []
                                psd = ps_g if wi == 0 else ps_u
                                gsl = ic % 2
                                C.wait(pe, ps_gu_free[gsl])
                                for k in range(DC):
                                    C.waits(pe, [xh_tok[k], xl_tok[k], wT_tok[dsl][k // 4]])
                                    pe.matmul(psd[:, gsl, :], whiT[:, dsl, k, :], xh[:, k, :],
                                              start=(k == 0), stop=False)
                                    pe.matmul(psd[:, gsl, :], whiT[:, dsl, k, :], xl[:, k, :],
                                              start=False, stop=False)
                                    mm = pe.matmul(psd[:, gsl, :], wloT[:, dsl, k, :], xh[:, k, :],
                                                   start=False, stop=(k == DC - 1))
                                tmm = C.inc(mm, "pe")
                                wT_free[dsl] = [tmm] + wT_tok[dsl][-1] + sp_tok
                                if wi == 0:
                                    tg = tmm
                                else:
                                    tu = tmm
                            gsl = ic % 2
                            C.waits(ae, [tg, sgb_free])
                            a = ae.activation(sgb[:], ps_g[:, gsl, :], AF.Silu)
                            tsg = C.inc(a, "ac")
                            C.waits(ve, [tsg, tu, htmp_free])
                            v = ve.tensor_tensor(htmp[:], sgb[:], ps_u[:, gsl, :], op=OP.mult)
                            th = C.inc(v, "dv")
                            sgb_free = th
                            ps_gu_free[gsl] = [tsg, th]
                            C.waits(ae, [th, h_free[icn]])
                            a = ae.activation(h_hi[:, icn, :], htmp[:], AF.Copy)
                            thi = C.inc(a, "ac")
                            C.waits(ve, [thi, h_free[icn]])
                            v = ve.tensor_tensor(h_lo[:, icn, :], htmp[:], h_hi[:, icn, :], op=OP.subtract)
                            tlo = C.inc(v, "dv")
                            htmp_free = [thi, tlo]
                            h_tok[icn] = [thi, tlo]
                        t_sb_last_mm = None
                        for dc in range(DC):
                            dsl = dc % 2
                            if half == 1:
                                th_rl = C.dma(se, f"rld{dsl}", dwhiT[:, dsl, 0:sbn, :],
                                              wspd[sbi, dc, 0, :, 0:sbn * 128],
                                              [dwT_free[dsl], sp_all])
                                tl_rl = C.dma(se, f"rld{dsl}", dwloT[:, dsl, 0:sbn, :],
                                              wspd[sbi, dc, 1, :, 0:sbn * 128], [])
                                dwT_tok[dsl] = [[tl_rl]] * 3
                                dsp_tok = []
                            else:
                                dwq_tok = C.dma(se, "dwq", dwq[:, 0:sbn * 128],
                                                dqw[dc * 128:(dc + 1) * 128, i0 * 128:(i0 + sbn) * 128],
                                                [dwq_free])
                                zsl = dc % 2
                                C.dma(se, f"dzs{zsl}", dzb[:, zsl, :],
                                      dzt[dc * 128:(dc + 1) * 128, :], [dzs_free[zsl]])
                                dzs_tok[zsl] = C.dma(se, f"dzs{zsl}", dsb[:, zsl, :],
                                                     dst_[dc * 128:(dc + 1) * 128, :], [])
                            dbatches = []
                            old_free = dwT_free[dsl]
                            tdq_last = None
                            ttps = []

                            def devac(b):
                                psl = b % 2
                                b0 = b * 4
                                nb = min(4, sbn - b0)
                                C.waits(ae, [ttps[b], old_free])
                                a = ae.activation(dwhiT[:, dsl, b0:b0 + nb, :],
                                                  ps_tr[:, psl, 0:nb * 128], AF.Copy)
                                thi = C.inc(a, "ac")
                                C.waits(ve, [thi, ttps[b], old_free])
                                v = ve.tensor_tensor(dwloT[:, dsl, b0:b0 + nb, :],
                                                     ps_tr[:, psl, 0:nb * 128],
                                                     dwhiT[:, dsl, b0:b0 + nb, :], op=OP.subtract)
                                tlo = C.inc(v, "dv")
                                dbatches.append([thi, tlo])
                                ring_free[psl] = [thi, tlo]

                            if half == 0:
                                for j in range(sbn):
                                    wsl = j % 4
                                    C.waits(ve, [dwq_tok, dzs_tok[zsl], wtile_free[wsl]])
                                    v = ve.tensor_scalar(wtile[:, wsl, :], dwq[:, j * 128:(j + 1) * 128],
                                                         dzb[:, zsl, i0 + j:i0 + j + 1],
                                                         dsb[:, zsl, i0 + j:i0 + j + 1],
                                                         op0=OP.subtract, op1=OP.mult)
                                    tdq = C.inc(v, "dv")
                                    tdq_last = tdq
                                    psl = (j // 4) % 2
                                    if j % 4 == 0:
                                        C.wait(pe, ring_free[psl])
                                    C.waits(pe, [tdq, tconst])
                                    mm = pe.transpose(ps_tr[:, psl, (j % 4) * 128:(j % 4 + 1) * 128],
                                                      wtile[:, wsl, :], ident[:])
                                    ttp = C.inc(mm, "pe")
                                    wtile_free[wsl] = ttp
                                    if j % 4 == 3 or j == sbn - 1:
                                        ttps.append(ttp)
                                        if len(ttps) >= 2:
                                            devac(len(ttps) - 2)
                                devac(len(ttps) - 1)
                                dwq_free = tdq_last
                                dzs_free[zsl] = tdq_last
                                dwT_tok[dsl] = dbatches
                                th_sp = C.dma(ge, f"spd{dsl}", wspd[sbi, dc, 0, :, 0:sbn * 128],
                                              dwhiT[:, dsl, 0:sbn, :], [dbatches[-1][0]])
                                tl_sp = C.dma(ge, f"spd{dsl}", wspd[sbi, dc, 1, :, 0:sbn * 128],
                                              dwloT[:, dsl, 0:sbn, :], [dbatches[-1][1]])
                                dsp_tok = [tl_sp]
                            psl_d = dc % 2
                            C.wait(pe, ps_d_free[psl_d])
                            for icn in range(sbn):
                                C.waits(pe, [h_tok[icn], dwT_tok[dsl][icn // 4]])
                                pe.matmul(ps_d[:, psl_d, :], dwhiT[:, dsl, icn, :], h_hi[:, icn, :],
                                          start=(icn == 0), stop=False)
                                pe.matmul(ps_d[:, psl_d, :], dwhiT[:, dsl, icn, :], h_lo[:, icn, :],
                                          start=False, stop=False)
                                mm = pe.matmul(ps_d[:, psl_d, :], dwloT[:, dsl, icn, :], h_hi[:, icn, :],
                                               start=False, stop=(icn == sbn - 1))
                            tmm = C.inc(mm, "pe")
                            dwT_free[dsl] = [tmm] + dwT_tok[dsl][-1] + dsp_tok
                            t_sb_last_mm = tmm
                            ssl = dc % 2
                            if sbi == 0:
                                C.waits(ve, [tmm, stgB_free[ssl]])
                                v = ve.tensor_copy(stg[:, ssl, :], ps_d[:, psl_d, :])
                                tev = C.inc(v, "dv")
                                ps_d_free[psl_d] = tev
                                tst = C.dma(ge, f"op{ssl}", pscr[half, 0, dc * 128:(dc + 1) * 128, :],
                                            stg[:, ssl, :], [tev])
                                stgB_free[ssl] = tst
                                py_tok[(half, dc)] = tst
                            elif sbi < NSB - 1:
                                tpl = C.dma(ge, f"pl{ssl}", plod[:, ssl, :],
                                            pscr[half, 0, dc * 128:(dc + 1) * 128, :],
                                            [plod_free[ssl], py_tok[(half, dc)]])
                                C.waits(ve, [tmm, tpl, stgB_free[ssl]])
                                v = ve.tensor_tensor(stg[:, ssl, :], ps_d[:, psl_d, :],
                                                     plod[:, ssl, :], op=OP.add)
                                tev = C.inc(v, "dv")
                                ps_d_free[psl_d] = tev
                                plod_free[ssl] = tev
                                tst = C.dma(ge, f"op{ssl}", pscr[half, 0, dc * 128:(dc + 1) * 128, :],
                                            stg[:, ssl, :], [tev])
                                stgB_free[ssl] = tst
                                py_tok[(half, dc)] = tst
                            else:
                                tpl = C.dma(ge, f"pl{ssl}", plod[:, ssl, :],
                                            pscr[half, 0, dc * 128:(dc + 1) * 128, :],
                                            [plod_free[ssl], py_tok[(half, dc)]])
                                C.waits(ve, [tmm, tpl, stgB_free[ssl]])
                                v = ve.tensor_tensor(stg[:, ssl, :], ps_d[:, psl_d, :],
                                                     plod[:, ssl, :], op=OP.add)
                                tacc = C.inc(v, "dv")
                                ps_d_free[psl_d] = tacc
                                tx2 = C.dma(ge, f"x2{ssl}", scr2[:, ssl, :],
                                            yt[dc * 128:(dc + 1) * 128, hc0:hc0 + HALF],
                                            [scr2_free[ssl]])
                                C.waits(ve, [tacc, tx2])
                                v = ve.tensor_tensor(plod[:, ssl, :], stg[:, ssl, :],
                                                     scr2[:, ssl, :], op=OP.add)
                                tacc2 = C.inc(v, "dv")
                                stgB_free[ssl] = tacc2
                                scr2_free[ssl] = [tacc2]
                                tst = C.dma(ge, f"oy{ssl}", yt[dc * 128:(dc + 1) * 128, hc0:hc0 + HALF],
                                            plod[:, ssl, :], [tacc2])
                                plod_free[ssl] = tst
                        for icn in range(11):
                            if h_tok[icn] is not None:
                                h_free[icn] = [t_sb_last_mm] + h_tok[icn][1:]
                barrier()
                for dc in range(DC):
                    xhl_free[dc] = C.tok("pe")
            for sname in ("oy0", "oy1", "op0", "op1"):
                C.wait(ge, C.tok(sname))
    return nc


def _host_prep(inputs):
    x = np.asarray(inputs["L0_attn_x"], dtype=np.float32)
    wqk = np.asarray(inputs["L0_attn_Wqk_w"], dtype=np.float32)
    bqk = np.asarray(inputs["L0_attn_Wqk_b"], dtype=np.float32)
    wv = np.asarray(inputs["L0_attn_Wv_w"], dtype=np.float32)
    bv = np.asarray(inputs["L0_attn_Wv_b"], dtype=np.float32)
    wo = np.asarray(inputs["L0_attn_Wo_w"], dtype=np.float32)
    bo = np.asarray(inputs["L0_attn_Wo_b"], dtype=np.float32)

    perm = np.zeros(1024, dtype=np.int64)
    e = np.arange(64)
    for hp in range(4):
        h0, h1 = 2 * hp, 2 * hp + 1
        base = hp * 256
        perm[base + 0:base + 64] = h0 * 128 + 2 * e
        perm[base + 64:base + 128] = h1 * 128 + 2 * e
        perm[base + 128:base + 192] = h0 * 128 + 2 * e + 1
        perm[base + 192:base + 256] = h1 * 128 + 2 * e + 1

    common = dict(
        wqkt=np.ascontiguousarray(wqk[perm].T),
        bqk=np.ascontiguousarray(bqk[perm].reshape(8, 128).T),
        wvt=np.ascontiguousarray(wv.T),
        bvr=np.ascontiguousarray(np.broadcast_to(bv[None, :], (128, 512))),
        wot=np.ascontiguousarray(wo.T),
        bo=np.ascontiguousarray(bo.reshape(32, 128).T),
        gq=np.ascontiguousarray(inputs["L0_ffn_gate_qweight"], dtype=np.int32),
        gz=np.ascontiguousarray(inputs["L0_ffn_gate_zeros"], dtype=np.float32),
        gs=np.ascontiguousarray(inputs["L0_ffn_gate_scales"], dtype=np.float32),
        uq=np.ascontiguousarray(inputs["L0_ffn_up_qweight"], dtype=np.int32),
        uz=np.ascontiguousarray(inputs["L0_ffn_up_zeros"], dtype=np.float32),
        us=np.ascontiguousarray(inputs["L0_ffn_up_scales"], dtype=np.float32),
        dqw=np.ascontiguousarray(inputs["L0_ffn_down_qweight"], dtype=np.int32),
        dzt=np.ascontiguousarray(inputs["L0_ffn_down_zeros"], dtype=np.float32),
        dst=np.ascontiguousarray(inputs["L0_ffn_down_scales"], dtype=np.float32),
        identd=np.eye(128, dtype=np.float32),
        onescd=np.ones((128, 1), dtype=np.float32),
        onesrd=np.ones((1, 128), dtype=np.float32),
    )

    in_maps = []
    for c in range(8):
        b, hf = c // 2, c % 2
        xb = x[b].T
        own = xb[:, hf * 1024:(hf + 1) * 1024]
        other = xb[:, (1 - hf) * 1024:(2 - hf) * 1024]
        im = dict(common)
        im["xt"] = np.ascontiguousarray(np.concatenate([own, other], axis=1))
        in_maps.append(im)
    return in_maps


def kernel(**inputs):
    in_maps = _host_prep(inputs)
    nc = build_nc()
    res = run_bass_kernel_spmd(nc, in_maps, core_ids=list(range(8)))
    out = np.empty((4, 2048, 4096), dtype=np.float32)
    for c in range(8):
        b, hf = c // 2, c % 2
        out[b, hf * 1024:(hf + 1) * 1024, :] = res.results[c]["yt"].T
    return out
